# version 20
# baseline (speedup 1.0000x reference)
import numpy as np
from contextlib import ExitStack

import concourse.bass as bass
import concourse.mybir as mybir
from concourse.bass_utils import run_bass_kernel_spmd
from concourse.alu_op_type import AluOpType

F32 = mybir.dt.float32
BF16 = mybir.dt.bfloat16
AF = mybir.ActivationFunctionType
AX = mybir.AxisListType

B, T, C = 16, 2048, 1024
NCORES = 8
SEQ = B // NCORES
TC = 512
NCH = T // TC
G = C // 128
H, DH = 16, 64
EPS = 1e-5

W_RES = ["wr", "wk", "wv"]
W_STREAM = ["wq", "wka", "wva", "wo"]
W_NAMES = W_RES + W_STREAM


class Mk:

    ENG_SEM = {"tensor": "t", "vector": "v", "scalar": "s", "gpsimd": "g"}

    def __init__(self):
        self.ops = {e: [] for e in ["sync", "tensor", "vector", "scalar", "gpsimd"]}
        self.counts = {}
        for s in ["t", "v", "s", "g", "dio", "dw", "dst"]:
            self.counts[s] = 0

    class Res:
        __slots__ = ("prod", "cons")

        def __init__(self):
            self.prod = None
            self.cons = {}

    def res(self):
        return Mk.Res()

    def emit(self, queue, fn, reads=(), writes=(), dma=None, extra_waits=()):
        kind = "d" if dma else "c"
        waits = {}
        for s_, v_ in extra_waits:
            if waits.get(s_, 0) < v_:
                waits[s_] = v_

        def need(marker):
            if marker is None:
                return
            sem, val, mq, mkind = marker
            if mq == queue and mkind == kind:
                return
            if waits.get(sem, 0) < val:
                waits[sem] = val

        for r in reads:
            need(r.prod)
        for w in writes:
            need(w.prod)
            for sem, (val, mq, mkind) in w.cons.items():
                need((sem, val, mq, mkind))

        if dma:
            sem_name, n = dma, 16
        else:
            sem_name, n = Mk.ENG_SEM[queue], 1
        self.counts[sem_name] += n
        marker = (sem_name, self.counts[sem_name], queue, kind)

        for s, v in waits.items():
            assert v <= self.counts[s], (s, v, self.counts[s])
        self.ops[queue].append((fn, sorted(waits.items()), sem_name, n))

        for r in reads:
            prev = r.cons.get(marker[0])
            if prev is None or prev[0] < marker[1]:
                r.cons[marker[0]] = (marker[1], queue, kind)
        for w in writes:
            w.prod = marker
            w.cons = {}
        return marker

    def realize(self, block, sems):
        for eng_name, ops in self.ops.items():
            if not ops:
                continue

            def make(ops_):
                def body(eng):
                    last = {}
                    for fn, waits, sem_name, n in ops_:
                        for s, v in waits:
                            if last.get(s, 0) < v:
                                eng.wait_ge(sems[s], v)
                                last[s] = v
                        if fn is None:
                            continue
                        inst = fn(eng)
                        inst.then_inc(sems[sem_name], n)
                return body

            getattr(block, eng_name)(make(ops))


ALL_PHASES = ("ln1", "f1", "ln2", "qkv", "attn", "ln3", "f2")


def build_program(n_seq=SEQ, n_ch=NCH, phases=ALL_PHASES, dumps=()):
    nc = bass.Bass("TRN2", target_bir_lowering=False)

    x_h = nc.dram_tensor("x", [SEQ, C, T], F32, kind="ExternalInput").ap()
    out_h = nc.dram_tensor("out", [SEQ, C, T], F32, kind="ExternalOutput").ap()
    w_h = {n: nc.dram_tensor(n, [C, C], F32, kind="ExternalInput").ap() for n in W_NAMES}
    mur_h = nc.dram_tensor("mur", [128, G], F32, kind="ExternalInput").ap()
    muk_h = nc.dram_tensor("muk", [128, G], F32, kind="ExternalInput").ap()
    mf_h = nc.dram_tensor("mf", [128, 512], F32, kind="ExternalInput").ap()
    mh_h = nc.dram_tensor("mh", [128, 256], F32, kind="ExternalInput").ap()
    ml_h = nc.dram_tensor("ml", [128, 256], F32, kind="ExternalInput").ap()

    k = Mk()
    nchunks = n_seq * n_ch

    with ExitStack() as ctx:
        def sb(name, shape, dt):
            return ctx.enter_context(nc.sbuf_tensor(name, shape, dt))

        def ps(name):
            return ctx.enter_context(nc.psum_tensor(name, [128, 512], F32))

        wt = {n: sb("wt_" + n, [128, G * 1024], BF16) for n in W_RES}
        wb = [sb("wb0", [128, G * 1024], BF16), sb("wb1", [128, G * 1024], BF16)]
        xres2 = [sb("xresA", [128, G * TC], F32), sb("xresB", [128, G * TC], F32)]
        xhat = sb("xhat", [128, G * (TC + 1)], BF16)
        xrb = sb("xrb", [128, G * TC], BF16)
        qbuf = sb("qbuf", [128, G * TC], BF16)
        kact = sb("kact", [128, G * TC], BF16)
        kbuf = sb("kbuf", [128, G * (TC + 128)], BF16)
        vring = [sb(f"vring{i}", [128, 1024], BF16) for i in range(5)]
        ebuf = sb("ebuf", [128, 1024], BF16)
        meanb = sb("meanb", [128, 512], F32)
        varb = sb("varb", [128, 512], F32)
        bb = sb("bb", [128, 512], F32)
        tscr = sb("tscr", [128, 2 * 512], F32)
        uscr = sb("uscr", [128, 2 * (TC + 1)], BF16)
        dscr = sb("dscr", [128, 2 * 512], BF16)
        rscr = sb("rscr", [128, 2 * 512], F32)
        rscr2 = sb("rscr2", [128, 2 * 512], F32)
        fscr = sb("fscr", [128, 2 * 512], F32)
        xbf_s = sb("xbf_s", [128, 2 * 512], BF16)
        sq_s = sb("sq_s", [128, 2 * 512], BF16)
        rfull = sb("rfull", [128, G * TC], BF16)
        c1 = sb("c1", [128, G], BF16)
        c3 = sb("c3", [128, G], BF16)
        mur = sb("mur_t", [128, G], F32)
        muk = sb("muk_t", [128, G], F32)
        ones = sb("ones_t", [128, 128], BF16)
        zz = sb("zz_t", [128, 128], BF16)
        mf = sb("mf_t", [128, 512], BF16)
        mh = sb("mh_t", [128, 256], BF16)
        ml = sb("ml_t", [128, 256], BF16)
        c_eps = sb("c_eps", [128, 1], F32)
        c_nb4 = sb("c_nb4", [128, 1], F32)

        pj = [ps("pj0"), ps("pj1")]
        pst = [ps("pst0"), ps("pst1")]
        psT = [ps("psT0"), ps("psT1")]
        pdT = ps("pdT")
        pyT = ps("pyT")

        sems = {}
        for s in ["t", "v", "s", "g", "dio", "dw", "dst"]:
            sems[s] = ctx.enter_context(nc.semaphore(s))

        r_w = {n: k.res() for n in W_RES}
        r_wb = [k.res(), k.res()]
        r_xres2 = [[k.res() for _ in range(G)], [k.res() for _ in range(G)]]
        r_xhat = [k.res() for _ in range(G)]
        r_xr = [k.res() for _ in range(G)]
        r_q = [k.res() for _ in range(G)]
        r_kact = [k.res() for _ in range(G)]
        r_kbuf = [k.res() for _ in range(G)]
        r_vring = [k.res() for _ in range(5)]
        r_ebuf = [k.res(), k.res()]
        r_meanb, r_varb, r_bb = k.res(), k.res(), k.res()
        r_tscr = [k.res(), k.res()]
        r_uscr = [k.res(), k.res()]
        r_dscr = [k.res(), k.res()]
        r_rscr = [k.res(), k.res()]
        r_rscr2 = [k.res(), k.res()]
        r_fscr = [k.res(), k.res()]
        r_xbf = [k.res(), k.res()]
        r_sq = [k.res(), k.res()]
        r_rfull = [k.res() for _ in range(G)]
        r_c1, r_c3 = k.res(), k.res()
        r_mu = k.res()
        r_const = k.res()
        r_pj = [k.res(), k.res()]
        r_pst = [k.res(), k.res()]
        r_psT = [k.res(), k.res()]
        r_pdT, r_pyT = k.res(), k.res()

        for n in W_RES:
            k.emit("gpsimd", lambda e, n=n: e.dma_start(out=wt[n][:], in_=w_h[n].rearrange(
                "(g p) d -> p g d", p=128)), writes=[r_w[n]], dma="dw")
        k.emit("sync", lambda e: e.dma_start(out=mur[:], in_=mur_h), writes=[r_mu], dma="dio")
        k.emit("sync", lambda e: e.dma_start(out=muk[:], in_=muk_h), writes=[r_mu], dma="dio")
        k.emit("gpsimd", lambda e: e.dma_start(out=mf[:], in_=mf_h), writes=[r_const], dma="dw")
        k.emit("gpsimd", lambda e: e.dma_start(out=mh[:], in_=mh_h), writes=[r_const], dma="dw")
        k.emit("gpsimd", lambda e: e.dma_start(out=ml[:], in_=ml_h), writes=[r_const], dma="dw")
        k.emit("vector", lambda e: e.memset(ones[:], 1.0), writes=[r_const])
        k.emit("vector", lambda e: e.memset(zz[:], 0.0), writes=[r_const])
        k.emit("vector", lambda e: e.memset(c_eps[:], EPS), writes=[r_const])
        k.emit("vector", lambda e: e.memset(c_nb4[:], -4.0), writes=[r_const])

        def wtile(buf, cg, d):
            return buf[:, 1024 * cg + 128 * d: 1024 * cg + 128 * (d + 1)]

        def ld_stream(name, slot):
            k.emit("gpsimd", lambda e, name=name, slot=slot: e.dma_start(
                out=wb[slot][:], in_=w_h[name].rearrange("(g p) d -> p g d", p=128)),
                writes=[r_wb[slot]], dma="dw")

        def ld_x(chg):
            par = chg % 2
            sq, ch = divmod(chg, n_ch)
            xsrc = x_h[sq].rearrange("(g p) t -> p g t", p=128)[:, :, TC * ch: TC * (ch + 1)]
            k.emit("sync", lambda e, xsrc=xsrc, par=par: e.dma_start(
                out=xres2[par][:].rearrange("p (g t) -> p g t", t=TC), in_=xsrc),
                writes=list(r_xres2[par]), dma="dio")

        def st_x(chg):
            par = chg % 2
            sq, ch = divmod(chg, n_ch)
            odst = out_h[sq].rearrange("(g p) t -> p g t", p=128)[:, :, TC * ch: TC * (ch + 1)]
            k.emit("sync", lambda e, odst=odst, par=par: e.dma_start(
                out=odst, in_=xres2[par][:].rearrange("p (g t) -> p g t", t=TC)),
                reads=list(r_xres2[par]), dma="dst")

        def ln_phase_g(which, chg, sbank=None):
            par = chg % 2
            xres, r_xres = xres2[par], r_xres2[par]
            first_of_seq = (chg % n_ch == 0)
            if sbank is None:
                s0, rs0, s1, rs1 = pst[0], r_pst[0], pst[1], r_pst[1]
            else:
                (s0, rs0), (s1, rs1) = sbank
            for g in range(G):
                sl = g % 2
                k.emit("scalar", lambda e, g=g, sl=sl, xres=xres: e.copy(
                    xbf_s[:, 512 * sl: 512 * (sl + 1)], xres[:, TC * g: TC * (g + 1)]),
                    reads=[r_xres[g]], writes=[r_xbf[sl]])
                k.emit("scalar", lambda e, g=g, sl=sl, xres=xres: e.activation(
                    sq_s[:, 512 * sl: 512 * (sl + 1)], xres[:, TC * g: TC * (g + 1)], AF.Square),
                    reads=[r_xres[g]], writes=[r_sq[sl]])
                k.emit("tensor", lambda e, g=g, sl=sl, s0=s0: e.matmul(
                    s0[:], ones[:], xbf_s[:, 512 * sl: 512 * (sl + 1)],
                    start=(g == 0), stop=(g == G - 1)),
                    reads=[r_xbf[sl], r_const], writes=[rs0])
                k.emit("tensor", lambda e, g=g, sl=sl, s1=s1: e.matmul(
                    s1[:], ones[:], sq_s[:, 512 * sl: 512 * (sl + 1)],
                    start=(g == 0), stop=(g == G - 1)),
                    reads=[r_sq[sl], r_const], writes=[rs1])
                if g % 2 == 1:
                    yield
            k.emit("vector", lambda e, s0=s0: e.tensor_scalar_mul(
                meanb[:], s0[:], 1.0 / C),
                reads=[rs0], writes=[r_meanb])
            k.emit("vector", lambda e: e.tensor_tensor(
                tscr[:, 0:512], meanb[:], meanb[:], op=AluOpType.mult),
                reads=[r_meanb], writes=[r_tscr[0]])
            k.emit("vector", lambda e, s1=s1: e.scalar_tensor_tensor(
                varb[:], in0=s1[:], scalar=1.0 / C, in1=tscr[:, 0:512],
                op0=AluOpType.mult, op1=AluOpType.subtract),
                reads=[rs1, r_tscr[0]], writes=[r_varb])
            k.emit("scalar", lambda e: e.activation(
                bb[:], varb[:], AF.Ln, bias=c_eps[:], scale=1.0),
                reads=[r_varb, r_const], writes=[r_bb])
            k.emit("scalar", lambda e: e.activation(
                varb[:], bb[:], AF.Exp, scale=-0.5),
                reads=[r_bb], writes=[r_varb])
            k.emit("vector", lambda e: e.tensor_tensor(
                bb[:], meanb[:], varb[:], op=AluOpType.mult),
                reads=[r_meanb, r_varb], writes=[r_bb])
            yield

            if which == 2:
                for g in range(G):
                    sl = g % 2
                    aeng = "vector" if sl == 0 else "gpsimd"
                    k.emit(aeng, lambda e, g=g, sl=sl, xres=xres: e.tensor_tensor(
                        tscr[:, 512 * sl: 512 * (sl + 1)], xres[:, TC * g: TC * (g + 1)],
                        varb[:], op=AluOpType.mult),
                        reads=[r_xres[g], r_varb], writes=[r_tscr[sl]])
                    k.emit(aeng, lambda e, g=g, sl=sl: e.tensor_tensor(
                        xhat[:, (TC + 1) * g + 1: (TC + 1) * (g + 1)],
                        tscr[:, 512 * sl: 512 * (sl + 1)], bb[:], op=AluOpType.subtract),
                        reads=[r_tscr[sl], r_bb], writes=[r_xhat[g]])
                    yield
                return

            carry, r_carry = (c1, r_c1) if which == 1 else (c3, r_c3)
            if first_of_seq:
                k.emit("vector", lambda e, carry=carry: e.memset(carry[:], 0.0),
                       writes=[r_carry])
            for g in range(G):
                sl = g % 2
                u0 = 513 * sl
                aeng = "vector" if sl == 0 else "gpsimd"
                k.emit(aeng, lambda e, g=g, sl=sl, xres=xres: e.tensor_tensor(
                    tscr[:, 512 * sl: 512 * (sl + 1)], xres[:, TC * g: TC * (g + 1)],
                    varb[:], op=AluOpType.mult),
                    reads=[r_xres[g], r_varb], writes=[r_tscr[sl]])
                k.emit(aeng, lambda e, sl=sl, u0=u0: e.tensor_tensor(
                    uscr[:, u0 + 1: u0 + 513], tscr[:, 512 * sl: 512 * (sl + 1)],
                    bb[:], op=AluOpType.subtract),
                    reads=[r_tscr[sl], r_bb], writes=[r_uscr[sl]])
                k.emit(aeng, lambda e, g=g, u0=u0, sl=sl, carry=carry: e.tensor_copy(
                    uscr[:, u0: u0 + 1], carry[:, g: g + 1]),
                    reads=[r_carry], writes=[r_uscr[sl]])
                k.emit(aeng, lambda e, g=g, u0=u0, sl=sl, carry=carry: e.tensor_copy(
                    carry[:, g: g + 1], uscr[:, u0 + 512: u0 + 513]),
                    reads=[r_uscr[sl]], writes=[r_carry])
                k.emit("vector", lambda e, sl=sl, u0=u0: e.tensor_tensor(
                    dscr[:, 512 * sl: 512 * (sl + 1)], uscr[:, u0 + 1: u0 + 513],
                    uscr[:, u0: u0 + 512], op=AluOpType.subtract),
                    reads=[r_uscr[sl]], writes=[r_dscr[sl]])
                k.emit("vector", lambda e, g=g, sl=sl, u0=u0: e.scalar_tensor_tensor(
                    xrb[:, TC * g: TC * (g + 1)], in0=dscr[:, 512 * sl: 512 * (sl + 1)],
                    scalar=mur[:, g: g + 1], in1=uscr[:, u0: u0 + 512],
                    op0=AluOpType.mult, op1=AluOpType.add),
                    reads=[r_dscr[sl], r_uscr[sl], r_mu], writes=[r_xr[g]])
                k.emit("vector", lambda e, g=g, sl=sl, u0=u0: e.scalar_tensor_tensor(
                    rfull[:, TC * g: TC * (g + 1)], in0=dscr[:, 512 * sl: 512 * (sl + 1)],
                    scalar=muk[:, g: g + 1], in1=uscr[:, u0: u0 + 512],
                    op0=AluOpType.mult, op1=AluOpType.add),
                    reads=[r_dscr[sl], r_uscr[sl], r_mu], writes=[r_rfull[g]])
                yield

        def proj_g(wbuf, r_wbuf, r_rhs, rhs_tiles, banks, epilogue, gfirst=False):
            nb = len(banks)

            def mm(d, g):
                bank, r_bank = banks[d % nb]
                k.emit("tensor", lambda e, d=d, g=g, bank=bank, wbuf=wbuf,
                       rhs=rhs_tiles[g]: e.matmul(
                    bank[:], wtile(wbuf, g, d), rhs,
                    start=(g == 0), stop=(g == G - 1)),
                    reads=[r_rhs[g], r_wbuf], writes=[r_bank])

            if gfirst:
                for g in range(G):
                    mm(0, g)
                    mm(1, g)
                epilogue(0, *banks[0])
                epilogue(1, *banks[1])
                d0 = 2
                yield
            else:
                d0 = 0
            for d in range(d0, G):
                for g in range(G):
                    mm(d, g)
                epilogue(d, *banks[d % nb])
                yield

        def ffn_a_g(chg, kbuf_t, r_kbuf_t, scr, r_scr):
            xk_tiles = [rfull[:, TC * g: TC * (g + 1)] for g in range(G)]

            def ep_kact(d, bank, r_bank):
                sl = d % 2
                k.emit("scalar", lambda e, bank=bank, sl=sl, scr=scr: e.activation(
                    scr[:, 512 * sl: 512 * (sl + 1)], bank[:], AF.Relu),
                    reads=[r_bank], writes=[r_scr[sl]])
                k.emit("vector", lambda e, d=d, sl=sl, scr=scr, kbuf_t=kbuf_t: e.tensor_tensor(
                    kbuf_t[:, TC * d: TC * (d + 1)], scr[:, 512 * sl: 512 * (sl + 1)],
                    scr[:, 512 * sl: 512 * (sl + 1)], op=AluOpType.mult),
                    reads=[r_scr[sl]], writes=[r_kbuf_t[d]])

            banks = main_banks if kbuf_t is kact else pre_banks
            yield from proj_g(wt["wk"], r_w["wk"], r_rfull, xk_tiles, banks, ep_kact, gfirst=True)

        def ffn_gates_g(chg, banks, scr, r_scr):
            xr_tiles = [xrb[:, TC * g: TC * (g + 1)] for g in range(G)]
            for d in range(G):
                sl = d % 2
                br, r_br = banks[d % len(banks)]
                for g in range(G):
                    k.emit("tensor", lambda e, d=d, g=g, br=br, rhs=xr_tiles[g]: e.matmul(
                        br[:], wtile(wt["wr"], g, d), rhs,
                        start=(g == 0), stop=(g == G - 1)),
                        reads=[r_xr[g], r_w["wr"]], writes=[r_br])
                k.emit("scalar", lambda e, br=br, sl=sl: e.activation(
                    dscr[:, 512 * sl: 512 * (sl + 1)], br[:], AF.Tanh, scale=0.5),
                    reads=[r_br], writes=[r_dscr[sl]])
                k.emit("vector", lambda e, d=d, sl=sl: e.tensor_scalar(
                    rfull[:, TC * d: TC * (d + 1)], in0=dscr[:, 512 * sl: 512 * (sl + 1)],
                    scalar1=0.25, scalar2=0.25, op0=AluOpType.mult, op1=AluOpType.add),
                    reads=[r_dscr[sl]], writes=[r_rfull[d]])
                yield

        def ffn_vb_g(chg, kbuf_t, r_kbuf_t, banks):
            par = chg % 2
            xres, r_xres = xres2[par], r_xres2[par]
            kact_tiles = [kbuf_t[:, TC * g: TC * (g + 1)] for g in range(G)]
            for d in range(G):
                sl = d % 2
                bv, r_bv = banks[d % len(banks)]
                for g in range(G):
                    k.emit("tensor", lambda e, d=d, g=g, bv=bv, rhs=kact_tiles[g]: e.matmul(
                        bv[:], wtile(wt["wv"], g, d), rhs,
                        start=(g == 0), stop=(g == G - 1)),
                        reads=[r_kbuf_t[g], r_w["wv"]], writes=[r_bv])
                k.emit("vector", lambda e, d=d, bv=bv, sl=sl: e.tensor_tensor(
                    fscr[:, 512 * sl: 512 * (sl + 1)], rfull[:, TC * d: TC * (d + 1)],
                    bv[:], op=AluOpType.mult),
                    reads=[r_rfull[d], r_bv], writes=[r_fscr[sl]])
                k.emit("gpsimd", lambda e, d=d, sl=sl, xres=xres: e.tensor_tensor(
                    xres[:, TC * d: TC * (d + 1)], xres[:, TC * d: TC * (d + 1)],
                    fscr[:, 512 * sl: 512 * (sl + 1)], op=AluOpType.add),
                    reads=[r_fscr[sl]], writes=[r_xres[d]])
                yield

        def drain(gen):
            for _ in gen:
                pass

        def ln_phase(which, chg):
            drain(ln_phase_g(which, chg))

        def proj(*a, **kw):
            drain(proj_g(*a, **kw))

        def qkv_phase(chg, lngen=iter(())):
            def proj_w(*a, **kw):
                for _ in proj_g(*a, **kw):
                    next(lngen, None)
            xh_tiles = [xhat[:, (TC + 1) * g + 1: (TC + 1) * (g + 1)] for g in range(G)]

            def ep_q(d, bank, r_bank):
                k.emit("vector", lambda e, d=d, bank=bank: e.tensor_copy(
                    qbuf[:, TC * d: TC * (d + 1)], bank[:]),
                    reads=[r_bank], writes=[r_q[d]])

            proj_w(wb[0], r_wb[0], r_xhat, xh_tiles, main_banks, ep_q, gfirst=True)
            ld_stream("wva", 0)

            if chg % n_ch != 0:
                k.emit("vector", lambda e: e.tensor_copy(
                    kbuf[:].rearrange("p (g c) -> p g c", c=TC + 128)[:, :, 0:128],
                    kbuf[:].rearrange("p (g c) -> p g c", c=TC + 128)[:, :, TC: TC + 128]),
                    reads=list(r_kbuf), writes=list(r_kbuf))

            def ep_k(d, bank, r_bank):
                k.emit("scalar", lambda e, d=d, bank=bank: e.copy(
                    kbuf[:, (TC + 128) * d + 128: (TC + 128) * (d + 1)], bank[:]),
                    reads=[r_bank], writes=[r_kbuf[d]])

            proj_w(wb[1], r_wb[1], r_xhat, xh_tiles, main_banks, ep_k)
            ld_stream("wo", 1)
            next(lngen, None)

            vtm_banks = [(pj[0], r_pj[0]), (pj[1], r_pj[1]),
                         (psT[0], r_psT[0]), (psT[1], r_psT[1])]
            for tt in range(NCH):
                slot = (NCH * chg + tt) % 5
                for half in range(2):
                    bank, r_bank = vtm_banks[(2 * tt + half) % 4]
                    for g in range(G):
                        k.emit("tensor", lambda e, tt=tt, half=half, g=g, bank=bank: e.matmul(
                            bank[:], xhat[:, (TC + 1) * g + 1 + 128 * tt:
                                          (TC + 1) * g + 1 + 128 * (tt + 1)],
                            wb[0][:, 1024 * g + 512 * half: 1024 * g + 512 * (half + 1)],
                            start=(g == 0), stop=(g == G - 1)),
                            reads=[r_xhat[g], r_wb[0]], writes=[r_bank])
                    k.emit("vector", lambda e, slot=slot, half=half, bank=bank: e.tensor_copy(
                        vring[slot][0:128, 512 * half: 512 * (half + 1)], bank[:]),
                        reads=[r_bank], writes=[r_vring[slot]])
                    next(lngen, None)

        def attn_phase(chg, weave=None):
            par = chg % 2
            xres, r_xres = xres2[par], r_xres2[par]
            first = (chg % n_ch == 0)
            units = ([] if first else [-1]) + [0, 1, 2, 3]
            sbanks = [(psT[0], r_psT[0]), (psT[1], r_psT[1]),
                      (pj[0], r_pj[0]), (pj[1], r_pj[1])]
            uctr = [0]
            pend = [None]
            zeroed = [False]

            def emit_scores(hg, jt):
                i = uctr[0]
                uctr[0] += 1
                slb, esl = i % 4, i % 2
                woff = max(jt, 0) * 128
                w = min(jt + 2, 4) * 128 - woff
                kcol = (TC + 128) * hg + 128 * (jt + 1)
                bnk, r_bnk = sbanks[slb]
                ma = k.emit("tensor", lambda e, bnk=bnk, kcol=kcol, woff=woff,
                            w=w, hg=hg: e.matmul(
                    bnk[:, 0: w],
                    kbuf[0:64, kcol: kcol + 128],
                    qbuf[0:64, TC * hg + woff: TC * hg + woff + w],
                    start=True, stop=True),
                    reads=[r_kbuf[hg], r_q[hg]], writes=[r_bnk])
                k.emit("tensor", lambda e, bnk=bnk, kcol=kcol, woff=woff,
                       w=w, hg=hg: e.matmul(
                    bnk[:, w: 2 * w],
                    kbuf[64:128, kcol: kcol + 128],
                    qbuf[64:128, TC * hg + woff: TC * hg + woff + w],
                    start=True, stop=True),
                    reads=[r_kbuf[hg], r_q[hg]], writes=[r_bnk],
                    extra_waits=[(ma[0], ma[1])])
                k.emit("scalar", lambda e, bnk=bnk, esl=esl, w=w: e.activation(
                    ebuf[:, 512 * esl: 512 * esl + 2 * w], bnk[:, 0: 2 * w],
                    AF.Exp, bias=c_nb4[:], scale=0.125),
                    reads=[r_bnk, r_const], writes=[r_ebuf[esl]])
                mask = mf[:, 0:512] if w == 256 else (mh[:] if jt == -1 else ml[:])
                k.emit("vector", lambda e, esl=esl, w=w, mask=mask: e.tensor_tensor(
                    ebuf[:, 512 * esl: 512 * esl + 2 * w],
                    ebuf[:, 512 * esl: 512 * esl + 2 * w], mask, op=AluOpType.mult),
                    reads=[r_ebuf[esl], r_const], writes=[r_ebuf[esl]])
                return (hg, jt, esl, woff, w)

            def emit_dpv(item):
                hg, jt, esl, woff, w = item
                if not zeroed[0]:
                    zeroed[0] = True
                    k.emit("tensor", lambda e: e.matmul(
                        pdT[:], zz[:], mf[:], start=True, stop=True),
                        reads=[r_const], writes=[r_pdT])
                    k.emit("tensor", lambda e: e.matmul(
                        pyT[:], zz[:], mf[:], start=True, stop=True),
                        reads=[r_const], writes=[r_pyT])
                vslot = (NCH * chg + jt) % 5 if jt >= 0 else (NCH * chg - 1) % 5
                for cb in range(w // 128):
                    qb = woff + 128 * cb
                    for hh in range(2):
                        tp = (0, 64 * hh)
                        eview = (512 * esl + w * hh + 128 * cb)
                        k.emit("tensor", lambda e, hh=hh, eview=eview, qb=qb, tp=tp: e.matmul(
                            pdT[64 * hh: 64 * (hh + 1), qb: qb + 128],
                            ones[:, 0:64], ebuf[:, eview: eview + 128],
                            start=False, stop=True, tile_position=tp),
                            reads=[r_ebuf[esl], r_const], writes=[r_pdT])
                        k.emit("tensor", lambda e, hh=hh, eview=eview, qb=qb, tp=tp,
                               vslot=vslot, hg=hg: e.matmul(
                            pyT[64 * hh: 64 * (hh + 1), qb: qb + 128],
                            vring[vslot][:, 64 * (2 * hg + hh): 64 * (2 * hg + hh + 1)],
                            ebuf[:, eview: eview + 128],
                            start=False, stop=True, tile_position=tp),
                            reads=[r_ebuf[esl], r_vring[vslot]], writes=[r_pyT])

            def emit_group_tail(hg):
                asl = hg % 2
                k.emit("scalar", lambda e, asl=asl: e.activation(
                    rscr[:, 512 * asl: 512 * (asl + 1)], pdT[:], AF.Ln),
                    reads=[r_pdT], writes=[r_rscr[asl]])
                k.emit("scalar", lambda e, asl=asl: e.activation(
                    rscr2[:, 512 * asl: 512 * (asl + 1)], rscr[:, 512 * asl: 512 * (asl + 1)],
                    AF.Exp, scale=-1.0),
                    reads=[r_rscr[asl]], writes=[r_rscr2[asl]])
                k.emit("vector", lambda e, hg=hg, asl=asl: e.tensor_tensor(
                    kact[:, TC * hg: TC * (hg + 1)], pyT[:],
                    rscr2[:, 512 * asl: 512 * (asl + 1)], op=AluOpType.mult),
                    reads=[r_pyT, r_rscr2[asl]], writes=[r_kact[hg]])

            uidx = [0]
            for hg in range(G):
                for jt in units:
                    item = emit_scores(hg, jt)
                    if pend[0] is not None:
                        ph = pend[0][0]
                        emit_dpv(pend[0])
                        if ph != hg:
                            emit_group_tail(ph)
                            zeroed[0] = False
                    pend[0] = item
                    if weave is not None:
                        weave(uidx[0])
                    uidx[0] += 1
            emit_dpv(pend[0])
            emit_group_tail(G - 1)
            pend[0] = None

            y_tiles = [kact[:, TC * g: TC * (g + 1)] for g in range(G)]

            def ep_wo(d, bank, r_bank):
                k.emit("vector", lambda e, d=d, bank=bank, xres=xres: e.tensor_tensor(
                    xres[:, TC * d: TC * (d + 1)], xres[:, TC * d: TC * (d + 1)],
                    bank[:], op=AluOpType.add),
                    reads=[r_bank], writes=[r_xres[d]])

            proj(wb[1], r_wb[1], r_kact, y_tiles,
                 [(pj[0], r_pj[0]), (pj[1], r_pj[1]), (psT[0], r_psT[0]), (psT[1], r_psT[1])],
                 ep_wo)

        def emit_dumps():
            if not dumps:
                return
            registry = {
                "xrb": (xrb, r_xr, G * TC), "xkb": (xkb, r_xk, G * TC),
                "kbuf": (kbuf, r_kbuf, G * (TC + 128)),
                "kact": (kact, r_kact, G * TC),
                "xhat": (xhat, r_xhat, G * (TC + 1)),
                "vring0": (vring[0], [r_vring[0]], 1024),
                "ebuf": (ebuf, r_ebuf, 1024),
            }
            dbg_view = out_h[1].rearrange("(g p) t -> p g t", p=128)
            gidx = 0
            for name in dumps:
                buf, rs, ncols = registry[name]
                for p0 in range(0, ncols, 2048):
                    plen = min(2048, ncols - p0)
                    k.emit("gpsimd", lambda e, buf=buf, p0=p0, plen=plen, gidx=gidx: e.dma_start(
                        out=dbg_view[:, gidx: gidx + 1, 0: plen],
                        in_=buf[:, p0: p0 + plen].rearrange("p (o t) -> p o t", o=1)),
                        reads=list(rs), dma="dst")
                    gidx += 1

        pre_banks = [(pst[0], r_pst[0]), (pst[1], r_pst[1])]
        main_banks = [(pj[0], r_pj[0]), (pj[1], r_pj[1]),
                      (psT[0], r_psT[0]), (psT[1], r_psT[1])]

        def prelude1_g(chg):
            if "f1" in phases:
                yield from ffn_a_g(chg, xhat, r_xhat, fscr, r_fscr)
                yield from ffn_gates_g(chg, pre_banks, fscr, r_fscr)
                yield from ffn_vb_g(chg, xhat, r_xhat, pre_banks)

        ld_x(0)
        ld_stream("wq", 0)
        ld_stream("wka", 1)
        if "ln1" in phases:
            drain(ln_phase_g(1, 0))
        drain(prelude1_g(0))
        for chg in range(nchunks):
            if chg + 1 < nchunks:
                ld_x(chg + 1)
            if "ln2" in phases:
                ln_phase(2, chg)
            lngen = (ln_phase_g(1, chg + 1)
                     if (chg + 1 < nchunks and "ln1" in phases) else iter(()))
            if "qkv" in phases:
                qkv_phase(chg, lngen)
            drain(lngen)
            pgen = prelude1_g(chg + 1) if chg + 1 < nchunks else iter(())
            wq_done = [False]

            def weave(uidx, pgen=pgen, wq_done=wq_done):
                if uidx == 2 and not wq_done[0]:
                    wq_done[0] = True
                    ld_stream("wq", 0)
                if uidx >= 2:
                    next(pgen, None)

            if "attn" in phases:
                attn_phase(chg, weave=weave)
                drain(pgen)
                if chg + 1 < nchunks:
                    ld_stream("wka", 1)
            else:
                if chg + 1 < nchunks:
                    ld_stream("wq", 0)
                    ld_stream("wka", 1)
                drain(pgen)
            if "ln3" in phases:
                ln_phase(3, chg)
            if "f2" in phases:
                drain(ffn_a_g(chg, kact, r_kact, rscr, r_rscr))
                drain(ffn_gates_g(chg, main_banks, rscr, r_rscr))
                drain(ffn_vb_g(chg, kact, r_kact, main_banks))
            st_x(chg)

        emit_dumps()

        total = k.counts["dst"]
        k.ops["sync"].append((None, [("dst", total)], None, 0))

        block = ctx.enter_context(nc.Block())
        k.realize(block, sems)

    return nc


_NC_CACHE = None


def _get_program():
    global _NC_CACHE
    if _NC_CACHE is None:
        _NC_CACHE = build_program()
    return _NC_CACHE


def kernel(L0_attn_x, L0_attn_Wq, L0_attn_Wk, L0_attn_Wv, L0_attn_Wo,
           L0_ffn_W_r, L0_ffn_W_k, L0_ffn_W_v, L0_ffn_mu_r, L0_ffn_mu_k):
    x = np.asarray(L0_attn_x, dtype=np.float32)

    def wt_(w):
        return np.ascontiguousarray(np.asarray(w, dtype=np.float32).T)

    shared = {
        "wr": wt_(L0_ffn_W_r), "wk": wt_(L0_ffn_W_k), "wv": wt_(L0_ffn_W_v),
        "wq": wt_(L0_attn_Wq), "wka": wt_(L0_attn_Wk), "wva": wt_(L0_attn_Wv),
        "wo": wt_(L0_attn_Wo),
        "mur": np.ascontiguousarray(np.asarray(L0_ffn_mu_r, np.float32).reshape(G, 128).T),
        "muk": np.ascontiguousarray(np.asarray(L0_ffn_mu_k, np.float32).reshape(G, 128).T),
    }
    r = np.arange(128)[:, None]
    c = np.arange(128)[None, :]
    triu = (c >= r).astype(np.float32)
    stril = (c < r).astype(np.float32)
    shared["mf"] = np.ascontiguousarray(np.concatenate([triu, stril, triu, stril], axis=1))
    shared["mh"] = np.ascontiguousarray(np.concatenate([stril, stril], axis=1))
    shared["ml"] = np.ascontiguousarray(np.concatenate([triu, triu], axis=1))

    in_maps = []
    for core in range(NCORES):
        xc = np.ascontiguousarray(x[SEQ * core: SEQ * (core + 1)].transpose(0, 2, 1))
        in_maps.append({"x": xc, **shared})

    nc = _get_program()
    res = run_bass_kernel_spmd(nc, in_maps, core_ids=list(range(NCORES)))
    outs = [np.asarray(res.results[i]["out"]).transpose(0, 2, 1) for i in range(NCORES)]
    return np.ascontiguousarray(np.concatenate(outs, axis=0)).astype(np.float32)


# version 21
# speedup vs baseline: 1.0900x; 1.0900x over previous
import numpy as np
from contextlib import ExitStack

import concourse.bass as bass
import concourse.mybir as mybir
from concourse.bass_utils import run_bass_kernel_spmd
from concourse.alu_op_type import AluOpType

F32 = mybir.dt.float32
F32R = mybir.dt.float32r
BF16 = mybir.dt.bfloat16
AF = mybir.ActivationFunctionType
AX = mybir.AxisListType

B, T, C = 16, 2048, 1024
NCORES = 8
SEQ = B // NCORES
TC = 512
NCH = T // TC
G = C // 128
H, DH = 16, 64
EPS = 1e-5

W_RES = ["wr", "wk", "wv"]
W_STREAM = ["wq", "wka", "wva", "wo"]
W_NAMES = W_RES + W_STREAM


class Mk:

    ENG_SEM = {"tensor": "t", "vector": "v", "scalar": "s", "gpsimd": "g"}

    def __init__(self):
        self.ops = {e: [] for e in ["sync", "tensor", "vector", "scalar", "gpsimd"]}
        self.counts = {}
        for s in ["t", "v", "s", "g", "dio", "dw", "dst"]:
            self.counts[s] = 0

    class Res:
        __slots__ = ("prod", "cons")

        def __init__(self):
            self.prod = None
            self.cons = {}

    def res(self):
        return Mk.Res()

    def emit(self, queue, fn, reads=(), writes=(), dma=None, extra_waits=()):
        kind = "d" if dma else "c"
        waits = {}
        for s_, v_ in extra_waits:
            if waits.get(s_, 0) < v_:
                waits[s_] = v_

        def need(marker):
            if marker is None:
                return
            sem, val, mq, mkind = marker
            if mq == queue and mkind == kind:
                return
            if waits.get(sem, 0) < val:
                waits[sem] = val

        for r in reads:
            need(r.prod)
        for w in writes:
            need(w.prod)
            for sem, (val, mq, mkind) in w.cons.items():
                need((sem, val, mq, mkind))

        if dma:
            sem_name, n = dma, 16
        else:
            sem_name, n = Mk.ENG_SEM[queue], 1
        self.counts[sem_name] += n
        marker = (sem_name, self.counts[sem_name], queue, kind)

        for s, v in waits.items():
            assert v <= self.counts[s], (s, v, self.counts[s])
        self.ops[queue].append((fn, sorted(waits.items()), sem_name, n))

        for r in reads:
            prev = r.cons.get(marker[0])
            if prev is None or prev[0] < marker[1]:
                r.cons[marker[0]] = (marker[1], queue, kind)
        for w in writes:
            w.prod = marker
            w.cons = {}
        return marker

    def realize(self, block, sems):
        for eng_name, ops in self.ops.items():
            if not ops:
                continue

            def make(ops_):
                def body(eng):
                    last = {}
                    for fn, waits, sem_name, n in ops_:
                        for s, v in waits:
                            if last.get(s, 0) < v:
                                eng.wait_ge(sems[s], v)
                                last[s] = v
                        if fn is None:
                            continue
                        inst = fn(eng)
                        inst.then_inc(sems[sem_name], n)
                return body

            getattr(block, eng_name)(make(ops))


ALL_PHASES = ("ln1", "f1", "ln2", "qkv", "attn", "ln3", "f2")


def build_program(n_seq=SEQ, n_ch=NCH, phases=ALL_PHASES, dumps=()):
    nc = bass.Bass("TRN2", target_bir_lowering=False)

    x_h = nc.dram_tensor("x", [SEQ, C, T], F32R, kind="ExternalInput").ap()
    out_h = nc.dram_tensor("out", [SEQ, C, T], F32R, kind="ExternalOutput").ap()
    w_h = {n: nc.dram_tensor(n, [C, C], F32, kind="ExternalInput").ap() for n in W_NAMES}
    mur_h = nc.dram_tensor("mur", [128, G], F32, kind="ExternalInput").ap()
    muk_h = nc.dram_tensor("muk", [128, G], F32, kind="ExternalInput").ap()
    mf_h = nc.dram_tensor("mf", [128, 512], F32, kind="ExternalInput").ap()
    mh_h = nc.dram_tensor("mh", [128, 256], F32, kind="ExternalInput").ap()
    ml_h = nc.dram_tensor("ml", [128, 256], F32, kind="ExternalInput").ap()
    onr_h = nc.dram_tensor("onr", [128, 128], F32R, kind="ExternalInput").ap()

    k = Mk()
    nchunks = n_seq * n_ch

    with ExitStack() as ctx:
        def sb(name, shape, dt):
            return ctx.enter_context(nc.sbuf_tensor(name, shape, dt))

        def ps(name):
            return ctx.enter_context(nc.psum_tensor(name, [128, 512], F32))

        wt = {n: sb("wt_" + n, [128, G * 1024], BF16) for n in W_RES}
        wb = [sb("wb0", [128, G * 1024], BF16), sb("wb1", [128, G * 1024], BF16)]
        xres2 = [sb("xresA", [128, G * TC], F32R), sb("xresB", [128, G * TC], F32R)]
        xhat = sb("xhat", [128, G * (TC + 1)], BF16)
        xrb = sb("xrb", [128, G * TC], BF16)
        qbuf = sb("qbuf", [128, G * TC], BF16)
        kact = sb("kact", [128, G * TC], BF16)
        kbuf = sb("kbuf", [128, G * (TC + 128)], BF16)
        vring = [sb(f"vring{i}", [128, 1024], BF16) for i in range(5)]
        ebuf = sb("ebuf", [128, 1024], BF16)
        meanb = sb("meanb", [128, 512], F32)
        varb = sb("varb", [128, 512], F32)
        bb = sb("bb", [128, 512], F32)
        tscr = sb("tscr", [128, 2 * 512], F32)
        uscr = sb("uscr", [128, 2 * (TC + 1)], BF16)
        dscr = sb("dscr", [128, 2 * 512], BF16)
        rscr = sb("rscr", [128, 2 * 512], F32)
        rscr2 = sb("rscr2", [128, 2 * 512], F32)
        fscr = sb("fscr", [128, 2 * 512], F32)
        sq_s = sb("sq_s", [128, 2 * 512], F32R)
        rfull = sb("rfull", [128, G * TC], BF16)
        c1 = sb("c1", [128, G], BF16)
        c3 = sb("c3", [128, G], BF16)
        mur = sb("mur_t", [128, G], F32)
        muk = sb("muk_t", [128, G], F32)
        ones = sb("ones_t", [128, 128], BF16)
        ones_r = sb("ones_r", [128, 128], F32R)
        zz = sb("zz_t", [128, 128], BF16)
        mf = sb("mf_t", [128, 512], BF16)
        mh = sb("mh_t", [128, 256], BF16)
        ml = sb("ml_t", [128, 256], BF16)
        c_eps = sb("c_eps", [128, 1], F32)
        c_nb4 = sb("c_nb4", [128, 1], F32)

        pj = [ps("pj0"), ps("pj1")]
        pst = [ps("pst0"), ps("pst1")]
        psT = [ps("psT0"), ps("psT1")]
        pdT = ps("pdT")
        pyT = ps("pyT")

        sems = {}
        for s in ["t", "v", "s", "g", "dio", "dw", "dst"]:
            sems[s] = ctx.enter_context(nc.semaphore(s))

        r_w = {n: k.res() for n in W_RES}
        r_wb = [k.res(), k.res()]
        r_xres2 = [[k.res() for _ in range(G)], [k.res() for _ in range(G)]]
        r_xhat = [k.res() for _ in range(G)]
        r_xr = [k.res() for _ in range(G)]
        r_q = [k.res() for _ in range(G)]
        r_kact = [k.res() for _ in range(G)]
        r_kbuf = [k.res() for _ in range(G)]
        r_vring = [k.res() for _ in range(5)]
        r_ebuf = [k.res(), k.res()]
        r_meanb, r_varb, r_bb = k.res(), k.res(), k.res()
        r_tscr = [k.res(), k.res()]
        r_uscr = [k.res(), k.res()]
        r_dscr = [k.res(), k.res()]
        r_rscr = [k.res(), k.res()]
        r_rscr2 = [k.res(), k.res()]
        r_fscr = [k.res(), k.res()]
        r_sq = [k.res(), k.res()]
        r_rfull = [k.res() for _ in range(G)]
        r_c1, r_c3 = k.res(), k.res()
        r_mu = k.res()
        r_const = k.res()
        r_pj = [k.res(), k.res()]
        r_pst = [k.res(), k.res()]
        r_psT = [k.res(), k.res()]
        r_pdT, r_pyT = k.res(), k.res()

        for n in W_RES:
            k.emit("gpsimd", lambda e, n=n: e.dma_start(out=wt[n][:], in_=w_h[n].rearrange(
                "(g p) d -> p g d", p=128)), writes=[r_w[n]], dma="dw")
        k.emit("sync", lambda e: e.dma_start(out=mur[:], in_=mur_h), writes=[r_mu], dma="dio")
        k.emit("sync", lambda e: e.dma_start(out=muk[:], in_=muk_h), writes=[r_mu], dma="dio")
        k.emit("gpsimd", lambda e: e.dma_start(out=mf[:], in_=mf_h), writes=[r_const], dma="dw")
        k.emit("gpsimd", lambda e: e.dma_start(out=mh[:], in_=mh_h), writes=[r_const], dma="dw")
        k.emit("gpsimd", lambda e: e.dma_start(out=ml[:], in_=ml_h), writes=[r_const], dma="dw")
        k.emit("vector", lambda e: e.memset(ones[:], 1.0), writes=[r_const])
        k.emit("sync", lambda e: e.dma_start(out=ones_r[:], in_=onr_h), writes=[r_const], dma="dio")
        k.emit("vector", lambda e: e.memset(zz[:], 0.0), writes=[r_const])
        k.emit("vector", lambda e: e.memset(c_eps[:], EPS), writes=[r_const])
        k.emit("vector", lambda e: e.memset(c_nb4[:], -4.0), writes=[r_const])

        def wtile(buf, cg, d):
            return buf[:, 1024 * cg + 128 * d: 1024 * cg + 128 * (d + 1)]

        def ld_stream(name, slot):
            k.emit("gpsimd", lambda e, name=name, slot=slot: e.dma_start(
                out=wb[slot][:], in_=w_h[name].rearrange("(g p) d -> p g d", p=128)),
                writes=[r_wb[slot]], dma="dw")

        def ld_x(chg):
            par = chg % 2
            sq, ch = divmod(chg, n_ch)
            xsrc = x_h[sq].rearrange("(g p) t -> p g t", p=128)[:, :, TC * ch: TC * (ch + 1)]
            k.emit("sync", lambda e, xsrc=xsrc, par=par: e.dma_start(
                out=xres2[par][:].rearrange("p (g t) -> p g t", t=TC), in_=xsrc),
                writes=list(r_xres2[par]), dma="dio")

        def st_x(chg):
            par = chg % 2
            sq, ch = divmod(chg, n_ch)
            odst = out_h[sq].rearrange("(g p) t -> p g t", p=128)[:, :, TC * ch: TC * (ch + 1)]
            k.emit("sync", lambda e, odst=odst, par=par: e.dma_start(
                out=odst, in_=xres2[par][:].rearrange("p (g t) -> p g t", t=TC)),
                reads=list(r_xres2[par]), dma="dst")

        def ln_phase_g(which, chg, sbank=None):
            par = chg % 2
            xres, r_xres = xres2[par], r_xres2[par]
            first_of_seq = (chg % n_ch == 0)
            if sbank is None:
                s0, rs0, s1, rs1 = pst[0], r_pst[0], pst[1], r_pst[1]
            else:
                (s0, rs0), (s1, rs1) = sbank
            for g in range(G):
                sl = g % 2
                k.emit("scalar", lambda e, g=g, sl=sl, xres=xres: e.activation(
                    sq_s[:, 512 * sl: 512 * (sl + 1)], xres[:, TC * g: TC * (g + 1)], AF.Square),
                    reads=[r_xres[g]], writes=[r_sq[sl]])
                k.emit("tensor", lambda e, g=g, s0=s0, xres=xres: e.matmul(
                    s0[:], ones_r[:], xres[:, TC * g: TC * (g + 1)],
                    start=(g == 0), stop=(g == G - 1)),
                    reads=[r_xres[g], r_const], writes=[rs0])
                k.emit("tensor", lambda e, g=g, sl=sl, s1=s1: e.matmul(
                    s1[:], ones_r[:], sq_s[:, 512 * sl: 512 * (sl + 1)],
                    start=(g == 0), stop=(g == G - 1)),
                    reads=[r_sq[sl], r_const], writes=[rs1])
                if g % 2 == 1:
                    yield
            k.emit("vector", lambda e, s0=s0: e.tensor_scalar_mul(
                meanb[:], s0[:], 1.0 / C),
                reads=[rs0], writes=[r_meanb])
            k.emit("vector", lambda e: e.tensor_tensor(
                tscr[:, 0:512], meanb[:], meanb[:], op=AluOpType.mult),
                reads=[r_meanb], writes=[r_tscr[0]])
            k.emit("vector", lambda e, s1=s1: e.scalar_tensor_tensor(
                varb[:], in0=s1[:], scalar=1.0 / C, in1=tscr[:, 0:512],
                op0=AluOpType.mult, op1=AluOpType.subtract),
                reads=[rs1, r_tscr[0]], writes=[r_varb])
            k.emit("scalar", lambda e: e.activation(
                bb[:], varb[:], AF.Ln, bias=c_eps[:], scale=1.0),
                reads=[r_varb, r_const], writes=[r_bb])
            k.emit("scalar", lambda e: e.activation(
                varb[:], bb[:], AF.Exp, scale=-0.5),
                reads=[r_bb], writes=[r_varb])
            k.emit("vector", lambda e: e.tensor_tensor(
                bb[:], meanb[:], varb[:], op=AluOpType.mult),
                reads=[r_meanb, r_varb], writes=[r_bb])
            yield

            if which == 2:
                for g in range(G):
                    sl = g % 2
                    k.emit("vector", lambda e, g=g, sl=sl, xres=xres: e.tensor_tensor(
                        tscr[:, 512 * sl: 512 * (sl + 1)], xres[:, TC * g: TC * (g + 1)],
                        varb[:], op=AluOpType.mult),
                        reads=[r_xres[g], r_varb], writes=[r_tscr[sl]])
                    k.emit("vector", lambda e, g=g, sl=sl: e.tensor_tensor(
                        xhat[:, (TC + 1) * g + 1: (TC + 1) * (g + 1)],
                        tscr[:, 512 * sl: 512 * (sl + 1)], bb[:], op=AluOpType.subtract),
                        reads=[r_tscr[sl], r_bb], writes=[r_xhat[g]])
                    yield
                return

            carry, r_carry = (c1, r_c1) if which == 1 else (c3, r_c3)
            if first_of_seq:
                k.emit("vector", lambda e, carry=carry: e.memset(carry[:], 0.0),
                       writes=[r_carry])
            aeng = "vector"
            for g in range(G):
                sl = g % 2
                u0 = 513 * sl
                k.emit(aeng, lambda e, g=g, sl=sl, xres=xres: e.tensor_tensor(
                    tscr[:, 512 * sl: 512 * (sl + 1)], xres[:, TC * g: TC * (g + 1)],
                    varb[:], op=AluOpType.mult),
                    reads=[r_xres[g], r_varb], writes=[r_tscr[sl]])
                k.emit(aeng, lambda e, sl=sl, u0=u0: e.tensor_tensor(
                    uscr[:, u0 + 1: u0 + 513], tscr[:, 512 * sl: 512 * (sl + 1)],
                    bb[:], op=AluOpType.subtract),
                    reads=[r_tscr[sl], r_bb], writes=[r_uscr[sl]])
                k.emit(aeng, lambda e, g=g, u0=u0, sl=sl, carry=carry: e.tensor_copy(
                    uscr[:, u0: u0 + 1], carry[:, g: g + 1]),
                    reads=[r_carry], writes=[r_uscr[sl]])
                k.emit(aeng, lambda e, g=g, u0=u0, sl=sl, carry=carry: e.tensor_copy(
                    carry[:, g: g + 1], uscr[:, u0 + 512: u0 + 513]),
                    reads=[r_uscr[sl]], writes=[r_carry])
                k.emit("vector", lambda e, sl=sl, u0=u0: e.tensor_tensor(
                    dscr[:, 512 * sl: 512 * (sl + 1)], uscr[:, u0 + 1: u0 + 513],
                    uscr[:, u0: u0 + 512], op=AluOpType.subtract),
                    reads=[r_uscr[sl]], writes=[r_dscr[sl]])
                k.emit("vector", lambda e, g=g, sl=sl, u0=u0: e.scalar_tensor_tensor(
                    xrb[:, TC * g: TC * (g + 1)], in0=dscr[:, 512 * sl: 512 * (sl + 1)],
                    scalar=mur[:, g: g + 1], in1=uscr[:, u0: u0 + 512],
                    op0=AluOpType.mult, op1=AluOpType.add),
                    reads=[r_dscr[sl], r_uscr[sl], r_mu], writes=[r_xr[g]])
                k.emit("vector", lambda e, g=g, sl=sl, u0=u0: e.scalar_tensor_tensor(
                    rfull[:, TC * g: TC * (g + 1)], in0=dscr[:, 512 * sl: 512 * (sl + 1)],
                    scalar=muk[:, g: g + 1], in1=uscr[:, u0: u0 + 512],
                    op0=AluOpType.mult, op1=AluOpType.add),
                    reads=[r_dscr[sl], r_uscr[sl], r_mu], writes=[r_rfull[g]])
                yield

        def proj_g(wbuf, r_wbuf, r_rhs, rhs_tiles, banks, epilogue, gfirst=False):
            nb = len(banks)

            def mm(d, g):
                bank, r_bank = banks[d % nb]
                k.emit("tensor", lambda e, d=d, g=g, bank=bank, wbuf=wbuf,
                       rhs=rhs_tiles[g]: e.matmul(
                    bank[:], wtile(wbuf, g, d), rhs,
                    start=(g == 0), stop=(g == G - 1)),
                    reads=[r_rhs[g], r_wbuf], writes=[r_bank])

            if gfirst:
                for g in range(G):
                    mm(0, g)
                    mm(1, g)
                epilogue(0, *banks[0])
                epilogue(1, *banks[1])
                d0 = 2
                yield
            else:
                d0 = 0
            for d in range(d0, G):
                for g in range(G):
                    mm(d, g)
                epilogue(d, *banks[d % nb])
                yield

        def ffn_a_g(chg, kbuf_t, r_kbuf_t, scr, r_scr):
            xk_tiles = [rfull[:, TC * g: TC * (g + 1)] for g in range(G)]

            def ep_kact(d, bank, r_bank):
                sl = d % 2
                k.emit("scalar", lambda e, bank=bank, sl=sl, scr=scr: e.activation(
                    scr[:, 512 * sl: 512 * (sl + 1)], bank[:], AF.Relu),
                    reads=[r_bank], writes=[r_scr[sl]])
                k.emit("vector", lambda e, d=d, sl=sl, scr=scr, kbuf_t=kbuf_t: e.tensor_tensor(
                    kbuf_t[:, TC * d: TC * (d + 1)], scr[:, 512 * sl: 512 * (sl + 1)],
                    scr[:, 512 * sl: 512 * (sl + 1)], op=AluOpType.mult),
                    reads=[r_scr[sl]], writes=[r_kbuf_t[d]])

            banks = main_banks if kbuf_t is kact else pre_banks
            yield from proj_g(wt["wk"], r_w["wk"], r_rfull, xk_tiles, banks, ep_kact, gfirst=True)

        def ffn_gates_g(chg, banks, scr, r_scr):
            xr_tiles = [xrb[:, TC * g: TC * (g + 1)] for g in range(G)]
            for d in range(G):
                sl = d % 2
                br, r_br = banks[d % len(banks)]
                for g in range(G):
                    k.emit("tensor", lambda e, d=d, g=g, br=br, rhs=xr_tiles[g]: e.matmul(
                        br[:], wtile(wt["wr"], g, d), rhs,
                        start=(g == 0), stop=(g == G - 1)),
                        reads=[r_xr[g], r_w["wr"]], writes=[r_br])
                k.emit("scalar", lambda e, br=br, sl=sl: e.activation(
                    dscr[:, 512 * sl: 512 * (sl + 1)], br[:], AF.Tanh, scale=0.5),
                    reads=[r_br], writes=[r_dscr[sl]])
                k.emit("vector", lambda e, d=d, sl=sl: e.tensor_scalar(
                    rfull[:, TC * d: TC * (d + 1)], in0=dscr[:, 512 * sl: 512 * (sl + 1)],
                    scalar1=0.25, scalar2=0.25, op0=AluOpType.mult, op1=AluOpType.add),
                    reads=[r_dscr[sl]], writes=[r_rfull[d]])
                yield

        def ffn_vb_g(chg, kbuf_t, r_kbuf_t, banks):
            par = chg % 2
            xres, r_xres = xres2[par], r_xres2[par]
            kact_tiles = [kbuf_t[:, TC * g: TC * (g + 1)] for g in range(G)]
            for d in range(G):
                sl = d % 2
                bv, r_bv = banks[d % len(banks)]
                for g in range(G):
                    k.emit("tensor", lambda e, d=d, g=g, bv=bv, rhs=kact_tiles[g]: e.matmul(
                        bv[:], wtile(wt["wv"], g, d), rhs,
                        start=(g == 0), stop=(g == G - 1)),
                        reads=[r_kbuf_t[g], r_w["wv"]], writes=[r_bv])
                k.emit("vector", lambda e, d=d, bv=bv, sl=sl: e.tensor_tensor(
                    fscr[:, 512 * sl: 512 * (sl + 1)], rfull[:, TC * d: TC * (d + 1)],
                    bv[:], op=AluOpType.mult),
                    reads=[r_rfull[d], r_bv], writes=[r_fscr[sl]])
                k.emit("gpsimd", lambda e, d=d, sl=sl, xres=xres: e.tensor_tensor(
                    xres[:, TC * d: TC * (d + 1)], xres[:, TC * d: TC * (d + 1)],
                    fscr[:, 512 * sl: 512 * (sl + 1)], op=AluOpType.add),
                    reads=[r_fscr[sl]], writes=[r_xres[d]])
                yield

        def drain(gen):
            for _ in gen:
                pass

        def ln_phase(which, chg):
            drain(ln_phase_g(which, chg))

        def proj(*a, **kw):
            drain(proj_g(*a, **kw))

        def qkv_phase(chg, lngen=iter(())):
            def proj_w(*a, **kw):
                for _ in proj_g(*a, **kw):
                    next(lngen, None)
            xh_tiles = [xhat[:, (TC + 1) * g + 1: (TC + 1) * (g + 1)] for g in range(G)]

            def ep_q(d, bank, r_bank):
                k.emit("vector", lambda e, d=d, bank=bank: e.tensor_copy(
                    qbuf[:, TC * d: TC * (d + 1)], bank[:]),
                    reads=[r_bank], writes=[r_q[d]])

            proj_w(wb[0], r_wb[0], r_xhat, xh_tiles, main_banks, ep_q, gfirst=True)
            ld_stream("wva", 0)

            if chg % n_ch != 0:
                k.emit("vector", lambda e: e.tensor_copy(
                    kbuf[:].rearrange("p (g c) -> p g c", c=TC + 128)[:, :, 0:128],
                    kbuf[:].rearrange("p (g c) -> p g c", c=TC + 128)[:, :, TC: TC + 128]),
                    reads=list(r_kbuf), writes=list(r_kbuf))

            def ep_k(d, bank, r_bank):
                k.emit("scalar", lambda e, d=d, bank=bank: e.copy(
                    kbuf[:, (TC + 128) * d + 128: (TC + 128) * (d + 1)], bank[:]),
                    reads=[r_bank], writes=[r_kbuf[d]])

            proj_w(wb[1], r_wb[1], r_xhat, xh_tiles, main_banks, ep_k)
            ld_stream("wo", 1)
            next(lngen, None)

            vtm_banks = [(pj[0], r_pj[0]), (pj[1], r_pj[1]),
                         (psT[0], r_psT[0]), (psT[1], r_psT[1])]
            for tt in range(NCH):
                slot = (NCH * chg + tt) % 5
                for half in range(2):
                    bank, r_bank = vtm_banks[(2 * tt + half) % 4]
                    for g in range(G):
                        k.emit("tensor", lambda e, tt=tt, half=half, g=g, bank=bank: e.matmul(
                            bank[:], xhat[:, (TC + 1) * g + 1 + 128 * tt:
                                          (TC + 1) * g + 1 + 128 * (tt + 1)],
                            wb[0][:, 1024 * g + 512 * half: 1024 * g + 512 * (half + 1)],
                            start=(g == 0), stop=(g == G - 1)),
                            reads=[r_xhat[g], r_wb[0]], writes=[r_bank])
                    k.emit("vector", lambda e, slot=slot, half=half, bank=bank: e.tensor_copy(
                        vring[slot][0:128, 512 * half: 512 * (half + 1)], bank[:]),
                        reads=[r_bank], writes=[r_vring[slot]])
                    next(lngen, None)

        def attn_phase(chg, weave=None):
            par = chg % 2
            xres, r_xres = xres2[par], r_xres2[par]
            first = (chg % n_ch == 0)
            units = ([] if first else [-1]) + [0, 1, 2, 3]
            sbanks = [(psT[0], r_psT[0]), (psT[1], r_psT[1]),
                      (pj[0], r_pj[0]), (pj[1], r_pj[1])]
            uctr = [0]
            pend = [None]
            zeroed = [False]

            def emit_scores(hg, jt):
                i = uctr[0]
                uctr[0] += 1
                slb, esl = i % 4, i % 2
                woff = max(jt, 0) * 128
                w = min(jt + 2, 4) * 128 - woff
                kcol = (TC + 128) * hg + 128 * (jt + 1)
                bnk, r_bnk = sbanks[slb]
                ma = k.emit("tensor", lambda e, bnk=bnk, kcol=kcol, woff=woff,
                            w=w, hg=hg: e.matmul(
                    bnk[:, 0: w],
                    kbuf[0:64, kcol: kcol + 128],
                    qbuf[0:64, TC * hg + woff: TC * hg + woff + w],
                    start=True, stop=True),
                    reads=[r_kbuf[hg], r_q[hg]], writes=[r_bnk])
                k.emit("tensor", lambda e, bnk=bnk, kcol=kcol, woff=woff,
                       w=w, hg=hg: e.matmul(
                    bnk[:, w: 2 * w],
                    kbuf[64:128, kcol: kcol + 128],
                    qbuf[64:128, TC * hg + woff: TC * hg + woff + w],
                    start=True, stop=True),
                    reads=[r_kbuf[hg], r_q[hg]], writes=[r_bnk],
                    extra_waits=[(ma[0], ma[1])])
                k.emit("scalar", lambda e, bnk=bnk, esl=esl, w=w: e.activation(
                    ebuf[:, 512 * esl: 512 * esl + 2 * w], bnk[:, 0: 2 * w],
                    AF.Exp, bias=c_nb4[:], scale=0.125),
                    reads=[r_bnk, r_const], writes=[r_ebuf[esl]])
                mask = mf[:, 0:512] if w == 256 else (mh[:] if jt == -1 else ml[:])
                k.emit("vector", lambda e, esl=esl, w=w, mask=mask: e.tensor_tensor(
                    ebuf[:, 512 * esl: 512 * esl + 2 * w],
                    ebuf[:, 512 * esl: 512 * esl + 2 * w], mask, op=AluOpType.mult),
                    reads=[r_ebuf[esl], r_const], writes=[r_ebuf[esl]])
                return (hg, jt, esl, woff, w)

            def emit_dpv(item):
                hg, jt, esl, woff, w = item
                if not zeroed[0]:
                    zeroed[0] = True
                    k.emit("tensor", lambda e: e.matmul(
                        pdT[:], zz[:], mf[:], start=True, stop=True),
                        reads=[r_const], writes=[r_pdT])
                    k.emit("tensor", lambda e: e.matmul(
                        pyT[:], zz[:], mf[:], start=True, stop=True),
                        reads=[r_const], writes=[r_pyT])
                vslot = (NCH * chg + jt) % 5 if jt >= 0 else (NCH * chg - 1) % 5
                for cb in range(w // 128):
                    qb = woff + 128 * cb
                    for hh in range(2):
                        tp = (0, 64 * hh)
                        eview = (512 * esl + w * hh + 128 * cb)
                        k.emit("tensor", lambda e, hh=hh, eview=eview, qb=qb, tp=tp: e.matmul(
                            pdT[64 * hh: 64 * (hh + 1), qb: qb + 128],
                            ones[:, 0:64], ebuf[:, eview: eview + 128],
                            start=False, stop=True, tile_position=tp),
                            reads=[r_ebuf[esl], r_const], writes=[r_pdT])
                        k.emit("tensor", lambda e, hh=hh, eview=eview, qb=qb, tp=tp,
                               vslot=vslot, hg=hg: e.matmul(
                            pyT[64 * hh: 64 * (hh + 1), qb: qb + 128],
                            vring[vslot][:, 64 * (2 * hg + hh): 64 * (2 * hg + hh + 1)],
                            ebuf[:, eview: eview + 128],
                            start=False, stop=True, tile_position=tp),
                            reads=[r_ebuf[esl], r_vring[vslot]], writes=[r_pyT])

            def emit_group_tail(hg):
                asl = hg % 2
                k.emit("scalar", lambda e, asl=asl: e.activation(
                    rscr[:, 512 * asl: 512 * (asl + 1)], pdT[:], AF.Ln),
                    reads=[r_pdT], writes=[r_rscr[asl]])
                k.emit("scalar", lambda e, asl=asl: e.activation(
                    rscr2[:, 512 * asl: 512 * (asl + 1)], rscr[:, 512 * asl: 512 * (asl + 1)],
                    AF.Exp, scale=-1.0),
                    reads=[r_rscr[asl]], writes=[r_rscr2[asl]])
                k.emit("vector", lambda e, hg=hg, asl=asl: e.tensor_tensor(
                    kact[:, TC * hg: TC * (hg + 1)], pyT[:],
                    rscr2[:, 512 * asl: 512 * (asl + 1)], op=AluOpType.mult),
                    reads=[r_pyT, r_rscr2[asl]], writes=[r_kact[hg]])

            uidx = [0]
            for hg in range(G):
                for jt in units:
                    item = emit_scores(hg, jt)
                    if pend[0] is not None:
                        ph = pend[0][0]
                        emit_dpv(pend[0])
                        if ph != hg:
                            emit_group_tail(ph)
                            zeroed[0] = False
                    pend[0] = item
                    if weave is not None:
                        weave(uidx[0])
                    uidx[0] += 1
            emit_dpv(pend[0])
            emit_group_tail(G - 1)
            pend[0] = None

            y_tiles = [kact[:, TC * g: TC * (g + 1)] for g in range(G)]

            def ep_wo(d, bank, r_bank):
                k.emit("vector", lambda e, d=d, bank=bank, xres=xres: e.tensor_tensor(
                    xres[:, TC * d: TC * (d + 1)], xres[:, TC * d: TC * (d + 1)],
                    bank[:], op=AluOpType.add),
                    reads=[r_bank], writes=[r_xres[d]])

            proj(wb[1], r_wb[1], r_kact, y_tiles,
                 [(pj[0], r_pj[0]), (pj[1], r_pj[1]), (psT[0], r_psT[0]), (psT[1], r_psT[1])],
                 ep_wo)

        def emit_dumps():
            if not dumps:
                return
            registry = {
                "xrb": (xrb, r_xr, G * TC), "xkb": (xkb, r_xk, G * TC),
                "kbuf": (kbuf, r_kbuf, G * (TC + 128)),
                "kact": (kact, r_kact, G * TC),
                "xhat": (xhat, r_xhat, G * (TC + 1)),
                "vring0": (vring[0], [r_vring[0]], 1024),
                "ebuf": (ebuf, r_ebuf, 1024),
            }
            dbg_view = out_h[1].rearrange("(g p) t -> p g t", p=128)
            gidx = 0
            for name in dumps:
                buf, rs, ncols = registry[name]
                for p0 in range(0, ncols, 2048):
                    plen = min(2048, ncols - p0)
                    k.emit("gpsimd", lambda e, buf=buf, p0=p0, plen=plen, gidx=gidx: e.dma_start(
                        out=dbg_view[:, gidx: gidx + 1, 0: plen],
                        in_=buf[:, p0: p0 + plen].rearrange("p (o t) -> p o t", o=1)),
                        reads=list(rs), dma="dst")
                    gidx += 1

        pre_banks = [(pst[0], r_pst[0]), (pst[1], r_pst[1])]
        main_banks = [(pj[0], r_pj[0]), (pj[1], r_pj[1]),
                      (psT[0], r_psT[0]), (psT[1], r_psT[1])]

        def prelude1_g(chg):
            if "f1" in phases:
                yield from ffn_a_g(chg, xhat, r_xhat, fscr, r_fscr)
                yield from ffn_gates_g(chg, pre_banks, fscr, r_fscr)
                yield from ffn_vb_g(chg, xhat, r_xhat, pre_banks)

        ld_x(0)
        ld_stream("wq", 0)
        ld_stream("wka", 1)
        if "ln1" in phases:
            drain(ln_phase_g(1, 0))
        drain(prelude1_g(0))
        for chg in range(nchunks):
            if chg + 1 < nchunks:
                ld_x(chg + 1)
            if "ln2" in phases:
                ln_phase(2, chg)
            lngen = (ln_phase_g(1, chg + 1)
                     if (chg + 1 < nchunks and "ln1" in phases) else iter(()))
            if "qkv" in phases:
                qkv_phase(chg, lngen)
            drain(lngen)
            pgen = prelude1_g(chg + 1) if chg + 1 < nchunks else iter(())
            wq_done = [False]

            def weave(uidx, pgen=pgen, wq_done=wq_done):
                if uidx == 2 and not wq_done[0]:
                    wq_done[0] = True
                    ld_stream("wq", 0)
                if uidx >= 2:
                    next(pgen, None)

            if "attn" in phases:
                attn_phase(chg, weave=weave)
                drain(pgen)
                if chg + 1 < nchunks:
                    ld_stream("wka", 1)
            else:
                if chg + 1 < nchunks:
                    ld_stream("wq", 0)
                    ld_stream("wka", 1)
                drain(pgen)
            if "ln3" in phases:
                ln_phase(3, chg)
            if "f2" in phases:
                drain(ffn_a_g(chg, kact, r_kact, rscr, r_rscr))
                drain(ffn_gates_g(chg, main_banks, rscr, r_rscr))
                drain(ffn_vb_g(chg, kact, r_kact, main_banks))
            st_x(chg)

        emit_dumps()

        total = k.counts["dst"]
        k.ops["sync"].append((None, [("dst", total)], None, 0))

        block = ctx.enter_context(nc.Block())
        k.realize(block, sems)

    return nc


_NC_CACHE = None


def _get_program():
    global _NC_CACHE
    if _NC_CACHE is None:
        _NC_CACHE = build_program()
    return _NC_CACHE


def kernel(L0_attn_x, L0_attn_Wq, L0_attn_Wk, L0_attn_Wv, L0_attn_Wo,
           L0_ffn_W_r, L0_ffn_W_k, L0_ffn_W_v, L0_ffn_mu_r, L0_ffn_mu_k):
    x = np.asarray(L0_attn_x, dtype=np.float32)

    def wt_(w):
        return np.ascontiguousarray(np.asarray(w, dtype=np.float32).T)

    shared = {
        "wr": wt_(L0_ffn_W_r), "wk": wt_(L0_ffn_W_k), "wv": wt_(L0_ffn_W_v),
        "wq": wt_(L0_attn_Wq), "wka": wt_(L0_attn_Wk), "wva": wt_(L0_attn_Wv),
        "wo": wt_(L0_attn_Wo),
        "mur": np.ascontiguousarray(np.asarray(L0_ffn_mu_r, np.float32).reshape(G, 128).T),
        "muk": np.ascontiguousarray(np.asarray(L0_ffn_mu_k, np.float32).reshape(G, 128).T),
    }
    r = np.arange(128)[:, None]
    c = np.arange(128)[None, :]
    triu = (c >= r).astype(np.float32)
    stril = (c < r).astype(np.float32)
    shared["mf"] = np.ascontiguousarray(np.concatenate([triu, stril, triu, stril], axis=1))
    shared["mh"] = np.ascontiguousarray(np.concatenate([stril, stril], axis=1))
    shared["ml"] = np.ascontiguousarray(np.concatenate([triu, triu], axis=1))
    shared["onr"] = np.ones((128, 128), np.float32)

    in_maps = []
    for core in range(NCORES):
        xc = np.ascontiguousarray(x[SEQ * core: SEQ * (core + 1)].transpose(0, 2, 1))
        in_maps.append({"x": xc, **shared})

    nc = _get_program()
    res = run_bass_kernel_spmd(nc, in_maps, core_ids=list(range(NCORES)))
    outs = [np.asarray(res.results[i]["out"]).transpose(0, 2, 1) for i in range(NCORES)]
    return np.ascontiguousarray(np.concatenate(outs, axis=0)).astype(np.float32)


# version 22
# speedup vs baseline: 1.1102x; 1.0185x over previous
import numpy as np
from contextlib import ExitStack

import concourse.bass as bass
import concourse.mybir as mybir
from concourse.bass_utils import run_bass_kernel_spmd
from concourse.alu_op_type import AluOpType

F32 = mybir.dt.float32
F32R = mybir.dt.float32r
BF16 = mybir.dt.bfloat16
AF = mybir.ActivationFunctionType
AX = mybir.AxisListType

B, T, C = 16, 2048, 1024
NCORES = 8
SEQ = B // NCORES
TC = 512
NCH = T // TC
G = C // 128
H, DH = 16, 64
EPS = 1e-5

W_RES = ["wr", "wk", "wv"]
W_STREAM = ["wq", "wka", "wva", "wo"]
W_NAMES = W_RES + W_STREAM


class Mk:

    ENG_SEM = {"tensor": "t", "vector": "v", "scalar": "s", "gpsimd": "g"}

    def __init__(self):
        self.ops = {e: [] for e in ["sync", "tensor", "vector", "scalar", "gpsimd"]}
        self.counts = {}
        for s in ["t", "v", "s", "g", "dio", "dw", "dw0", "dw1", "dst"]:
            self.counts[s] = 0

    class Res:
        __slots__ = ("prod", "cons")

        def __init__(self):
            self.prod = None
            self.cons = {}

    def res(self):
        return Mk.Res()

    def emit(self, queue, fn, reads=(), writes=(), dma=None, extra_waits=()):
        kind = "d" if dma else "c"
        waits = {}
        for s_, v_ in extra_waits:
            if waits.get(s_, 0) < v_:
                waits[s_] = v_

        def need(marker):
            if marker is None:
                return
            sem, val, mq, mkind = marker
            if mq == queue and mkind == kind:
                return
            if waits.get(sem, 0) < val:
                waits[sem] = val

        for r in reads:
            need(r.prod)
        for w in writes:
            need(w.prod)
            for sem, (val, mq, mkind) in w.cons.items():
                need((sem, val, mq, mkind))

        if dma:
            sem_name, n = dma, 16
        else:
            sem_name, n = Mk.ENG_SEM[queue], 1
        self.counts[sem_name] += n
        marker = (sem_name, self.counts[sem_name], queue, kind)

        for s, v in waits.items():
            assert v <= self.counts[s], (s, v, self.counts[s])
        self.ops[queue].append((fn, sorted(waits.items()), sem_name, n))

        for r in reads:
            prev = r.cons.get(marker[0])
            if prev is None or prev[0] < marker[1]:
                r.cons[marker[0]] = (marker[1], queue, kind)
        for w in writes:
            w.prod = marker
            w.cons = {}
        return marker

    def realize(self, block, sems):
        for eng_name, ops in self.ops.items():
            if not ops:
                continue

            def make(ops_):
                def body(eng):
                    last = {}
                    for fn, waits, sem_name, n in ops_:
                        for s, v in waits:
                            if last.get(s, 0) < v:
                                eng.wait_ge(sems[s], v)
                                last[s] = v
                        if fn is None:
                            continue
                        inst = fn(eng)
                        inst.then_inc(sems[sem_name], n)
                return body

            getattr(block, eng_name)(make(ops))


ALL_PHASES = ("ln1", "f1", "ln2", "qkv", "attn", "ln3", "f2")


def build_program(n_seq=SEQ, n_ch=NCH, phases=ALL_PHASES, dumps=()):
    nc = bass.Bass("TRN2", target_bir_lowering=False)

    x_h = nc.dram_tensor("x", [SEQ, C, T], F32R, kind="ExternalInput").ap()
    out_h = nc.dram_tensor("out", [SEQ, C, T], F32R, kind="ExternalOutput").ap()
    w_h = {n: nc.dram_tensor(n, [C, C], F32, kind="ExternalInput").ap() for n in W_NAMES}
    mur_h = nc.dram_tensor("mur", [128, G], F32, kind="ExternalInput").ap()
    muk_h = nc.dram_tensor("muk", [128, G], F32, kind="ExternalInput").ap()
    mf_h = nc.dram_tensor("mf", [128, 512], F32, kind="ExternalInput").ap()
    mh_h = nc.dram_tensor("mh", [128, 256], F32, kind="ExternalInput").ap()
    ml_h = nc.dram_tensor("ml", [128, 256], F32, kind="ExternalInput").ap()
    onr_h = nc.dram_tensor("onr", [128, 128], F32R, kind="ExternalInput").ap()

    k = Mk()
    nchunks = n_seq * n_ch

    with ExitStack() as ctx:
        def sb(name, shape, dt):
            return ctx.enter_context(nc.sbuf_tensor(name, shape, dt))

        def ps(name):
            return ctx.enter_context(nc.psum_tensor(name, [128, 512], F32))

        wt = {n: sb("wt_" + n, [128, G * 1024], BF16) for n in W_RES}
        wb = [sb("wb0", [128, G * 1024], BF16), sb("wb1", [128, G * 1024], BF16)]
        xres2 = [sb("xresA", [128, G * TC], F32R), sb("xresB", [128, G * TC], F32R)]
        xhat = sb("xhat", [128, G * (TC + 1)], BF16)
        xrb = sb("xrb", [128, G * TC], BF16)
        qbuf = sb("qbuf", [128, G * TC], BF16)
        kact = sb("kact", [128, G * TC], BF16)
        kbuf = sb("kbuf", [128, G * (TC + 128)], BF16)
        vring = [sb(f"vring{i}", [128, 1024], BF16) for i in range(5)]
        ebuf = sb("ebuf", [128, 1024], BF16)
        meanb = sb("meanb", [128, 512], F32)
        varb = sb("varb", [128, 512], F32)
        bb = sb("bb", [128, 512], F32)
        tscr = sb("tscr", [128, 2 * 512], F32)
        uscr = sb("uscr", [128, 2 * (TC + 1)], BF16)
        dscr = sb("dscr", [128, 2 * 512], BF16)
        rscr = sb("rscr", [128, 2 * 512], F32)
        rscr2 = sb("rscr2", [128, 2 * 512], F32)
        fscr = sb("fscr", [128, 2 * 512], F32)
        sq_s = sb("sq_s", [128, 2 * 512], F32R)
        rfull = sb("rfull", [128, G * TC], BF16)
        c1 = sb("c1", [128, G], BF16)
        c3 = sb("c3", [128, G], BF16)
        mur = sb("mur_t", [128, G], F32)
        muk = sb("muk_t", [128, G], F32)
        ones = sb("ones_t", [128, 128], BF16)
        ones_r = sb("ones_r", [128, 128], F32R)
        zz = sb("zz_t", [128, 128], BF16)
        mf = sb("mf_t", [128, 512], BF16)
        mh = sb("mh_t", [128, 256], BF16)
        ml = sb("ml_t", [128, 256], BF16)
        c_eps = sb("c_eps", [128, 1], F32)
        c_nb4 = sb("c_nb4", [128, 1], F32)

        pj = [ps("pj0"), ps("pj1")]
        pst = [ps("pst0"), ps("pst1")]
        psT = [ps("psT0"), ps("psT1")]
        pdT = ps("pdT")
        pyT = ps("pyT")

        sems = {}
        for s in ["t", "v", "s", "g", "dio", "dw", "dw0", "dw1", "dst"]:
            sems[s] = ctx.enter_context(nc.semaphore(s))

        r_w = {n: k.res() for n in W_RES}
        r_wb = [k.res(), k.res()]
        r_xres2 = [[k.res() for _ in range(G)], [k.res() for _ in range(G)]]
        r_xhat = [k.res() for _ in range(G)]
        r_xr = [k.res() for _ in range(G)]
        r_q = [k.res() for _ in range(G)]
        r_kact = [k.res() for _ in range(G)]
        r_kbuf = [k.res() for _ in range(G)]
        r_vring = [k.res() for _ in range(5)]
        r_ebuf = [k.res(), k.res()]
        r_meanb, r_varb, r_bb = k.res(), k.res(), k.res()
        r_tscr = [k.res(), k.res()]
        r_uscr = [k.res(), k.res()]
        r_dscr = [k.res(), k.res()]
        r_rscr = [k.res(), k.res()]
        r_rscr2 = [k.res(), k.res()]
        r_fscr = [k.res(), k.res()]
        r_sq = [k.res(), k.res()]
        r_rfull = [k.res() for _ in range(G)]
        r_c1, r_c3 = k.res(), k.res()
        r_mu = k.res()
        r_const = k.res()
        r_masks = k.res()
        r_onesr = k.res()
        r_pj = [k.res(), k.res()]
        r_pst = [k.res(), k.res()]
        r_psT = [k.res(), k.res()]
        r_pdT, r_pyT = k.res(), k.res()

        for n in W_RES:
            k.emit("gpsimd", lambda e, n=n: e.dma_start(out=wt[n][:], in_=w_h[n].rearrange(
                "(g p) d -> p g d", p=128)), writes=[r_w[n]], dma="dw")
        k.emit("sync", lambda e: e.dma_start(out=mur[:], in_=mur_h), writes=[r_mu], dma="dio")
        k.emit("sync", lambda e: e.dma_start(out=muk[:], in_=muk_h), writes=[r_mu], dma="dio")
        k.emit("gpsimd", lambda e: e.dma_start(out=mf[:], in_=mf_h), writes=[r_masks], dma="dw")
        k.emit("gpsimd", lambda e: e.dma_start(out=mh[:], in_=mh_h), writes=[r_masks], dma="dw")
        k.emit("gpsimd", lambda e: e.dma_start(out=ml[:], in_=ml_h), writes=[r_masks], dma="dw")
        all_dw = ("dw", k.counts["dw"], "gpsimd", "d")
        for res_ in [r_w[n_] for n_ in W_RES] + [r_masks]:
            res_.prod = all_dw
        k.emit("vector", lambda e: e.memset(ones[:], 1.0), writes=[r_const])
        k.emit("sync", lambda e: e.dma_start(out=ones_r[:], in_=onr_h), writes=[r_onesr], dma="dio")
        k.emit("vector", lambda e: e.memset(zz[:], 0.0), writes=[r_const])
        k.emit("vector", lambda e: e.memset(c_eps[:], EPS), writes=[r_const])
        k.emit("vector", lambda e: e.memset(c_nb4[:], -4.0), writes=[r_const])

        def wtile(buf, cg, d):
            return buf[:, 1024 * cg + 128 * d: 1024 * cg + 128 * (d + 1)]

        def ld_stream(name, slot):
            k.emit("gpsimd", lambda e, name=name, slot=slot: e.dma_start(
                out=wb[slot][:], in_=w_h[name].rearrange("(g p) d -> p g d", p=128)),
                writes=[r_wb[slot]], dma="dw" + str(slot))

        def ld_x(chg):
            par = chg % 2
            sq, ch = divmod(chg, n_ch)
            xsrc = x_h[sq].rearrange("(g p) t -> p g t", p=128)[:, :, TC * ch: TC * (ch + 1)]
            k.emit("sync", lambda e, xsrc=xsrc, par=par: e.dma_start(
                out=xres2[par][:].rearrange("p (g t) -> p g t", t=TC), in_=xsrc),
                writes=list(r_xres2[par]), dma="dio")

        def st_x(chg):
            par = chg % 2
            sq, ch = divmod(chg, n_ch)
            odst = out_h[sq].rearrange("(g p) t -> p g t", p=128)[:, :, TC * ch: TC * (ch + 1)]
            k.emit("sync", lambda e, odst=odst, par=par: e.dma_start(
                out=odst, in_=xres2[par][:].rearrange("p (g t) -> p g t", t=TC)),
                reads=list(r_xres2[par]), dma="dst")

        def ln_phase_g(which, chg, sbank=None):
            par = chg % 2
            xres, r_xres = xres2[par], r_xres2[par]
            first_of_seq = (chg % n_ch == 0)
            if sbank is None:
                s0, rs0, s1, rs1 = pst[0], r_pst[0], pst[1], r_pst[1]
            else:
                (s0, rs0), (s1, rs1) = sbank
            for g in range(G):
                sl = g % 2
                k.emit("scalar", lambda e, g=g, sl=sl, xres=xres: e.activation(
                    sq_s[:, 512 * sl: 512 * (sl + 1)], xres[:, TC * g: TC * (g + 1)], AF.Square),
                    reads=[r_xres[g]], writes=[r_sq[sl]])
                k.emit("tensor", lambda e, g=g, s0=s0, xres=xres: e.matmul(
                    s0[:], ones_r[:], xres[:, TC * g: TC * (g + 1)],
                    start=(g == 0), stop=(g == G - 1)),
                    reads=[r_xres[g], r_onesr], writes=[rs0])
                k.emit("tensor", lambda e, g=g, sl=sl, s1=s1: e.matmul(
                    s1[:], ones_r[:], sq_s[:, 512 * sl: 512 * (sl + 1)],
                    start=(g == 0), stop=(g == G - 1)),
                    reads=[r_sq[sl], r_onesr], writes=[rs1])
                if g % 2 == 1:
                    yield
            k.emit("vector", lambda e, s0=s0: e.tensor_scalar_mul(
                meanb[:], s0[:], 1.0 / C),
                reads=[rs0], writes=[r_meanb])
            k.emit("vector", lambda e: e.tensor_tensor(
                tscr[:, 0:512], meanb[:], meanb[:], op=AluOpType.mult),
                reads=[r_meanb], writes=[r_tscr[0]])
            k.emit("vector", lambda e, s1=s1: e.scalar_tensor_tensor(
                varb[:], in0=s1[:], scalar=1.0 / C, in1=tscr[:, 0:512],
                op0=AluOpType.mult, op1=AluOpType.subtract),
                reads=[rs1, r_tscr[0]], writes=[r_varb])
            k.emit("scalar", lambda e: e.activation(
                bb[:], varb[:], AF.Ln, bias=c_eps[:], scale=1.0),
                reads=[r_varb, r_const], writes=[r_bb])
            k.emit("scalar", lambda e: e.activation(
                varb[:], bb[:], AF.Exp, scale=-0.5),
                reads=[r_bb], writes=[r_varb])
            k.emit("vector", lambda e: e.tensor_tensor(
                bb[:], meanb[:], varb[:], op=AluOpType.mult),
                reads=[r_meanb, r_varb], writes=[r_bb])
            yield

            if which == 2:
                for g in range(G):
                    sl = g % 2
                    k.emit("vector", lambda e, g=g, sl=sl, xres=xres: e.tensor_tensor(
                        tscr[:, 512 * sl: 512 * (sl + 1)], xres[:, TC * g: TC * (g + 1)],
                        varb[:], op=AluOpType.mult),
                        reads=[r_xres[g], r_varb], writes=[r_tscr[sl]])
                    k.emit("vector", lambda e, g=g, sl=sl: e.tensor_tensor(
                        xhat[:, (TC + 1) * g + 1: (TC + 1) * (g + 1)],
                        tscr[:, 512 * sl: 512 * (sl + 1)], bb[:], op=AluOpType.subtract),
                        reads=[r_tscr[sl], r_bb], writes=[r_xhat[g]])
                    yield
                return

            carry, r_carry = (c1, r_c1) if which == 1 else (c3, r_c3)
            if first_of_seq:
                k.emit("vector", lambda e, carry=carry: e.memset(carry[:], 0.0),
                       writes=[r_carry])
            aeng = "vector"
            for g in range(G):
                sl = g % 2
                u0 = 513 * sl
                k.emit(aeng, lambda e, g=g, sl=sl, xres=xres: e.tensor_tensor(
                    tscr[:, 512 * sl: 512 * (sl + 1)], xres[:, TC * g: TC * (g + 1)],
                    varb[:], op=AluOpType.mult),
                    reads=[r_xres[g], r_varb], writes=[r_tscr[sl]])
                k.emit(aeng, lambda e, sl=sl, u0=u0: e.tensor_tensor(
                    uscr[:, u0 + 1: u0 + 513], tscr[:, 512 * sl: 512 * (sl + 1)],
                    bb[:], op=AluOpType.subtract),
                    reads=[r_tscr[sl], r_bb], writes=[r_uscr[sl]])
                k.emit(aeng, lambda e, g=g, u0=u0, sl=sl, carry=carry: e.tensor_copy(
                    uscr[:, u0: u0 + 1], carry[:, g: g + 1]),
                    reads=[r_carry], writes=[r_uscr[sl]])
                k.emit(aeng, lambda e, g=g, u0=u0, sl=sl, carry=carry: e.tensor_copy(
                    carry[:, g: g + 1], uscr[:, u0 + 512: u0 + 513]),
                    reads=[r_uscr[sl]], writes=[r_carry])
                k.emit("vector", lambda e, sl=sl, u0=u0: e.tensor_tensor(
                    dscr[:, 512 * sl: 512 * (sl + 1)], uscr[:, u0 + 1: u0 + 513],
                    uscr[:, u0: u0 + 512], op=AluOpType.subtract),
                    reads=[r_uscr[sl]], writes=[r_dscr[sl]])
                k.emit("vector", lambda e, g=g, sl=sl, u0=u0: e.scalar_tensor_tensor(
                    xrb[:, TC * g: TC * (g + 1)], in0=dscr[:, 512 * sl: 512 * (sl + 1)],
                    scalar=mur[:, g: g + 1], in1=uscr[:, u0: u0 + 512],
                    op0=AluOpType.mult, op1=AluOpType.add),
                    reads=[r_dscr[sl], r_uscr[sl], r_mu], writes=[r_xr[g]])
                k.emit("vector", lambda e, g=g, sl=sl, u0=u0: e.scalar_tensor_tensor(
                    rfull[:, TC * g: TC * (g + 1)], in0=dscr[:, 512 * sl: 512 * (sl + 1)],
                    scalar=muk[:, g: g + 1], in1=uscr[:, u0: u0 + 512],
                    op0=AluOpType.mult, op1=AluOpType.add),
                    reads=[r_dscr[sl], r_uscr[sl], r_mu], writes=[r_rfull[g]])
                yield

        def proj_g(wbuf, r_wbuf, r_rhs, rhs_tiles, banks, epilogue, gfirst=False):
            nb = len(banks)

            def mm(d, g):
                bank, r_bank = banks[d % nb]
                k.emit("tensor", lambda e, d=d, g=g, bank=bank, wbuf=wbuf,
                       rhs=rhs_tiles[g]: e.matmul(
                    bank[:], wtile(wbuf, g, d), rhs,
                    start=(g == 0), stop=(g == G - 1)),
                    reads=[r_rhs[g], r_wbuf], writes=[r_bank])

            if gfirst:
                for g in range(G):
                    mm(0, g)
                    mm(1, g)
                epilogue(0, *banks[0])
                epilogue(1, *banks[1])
                d0 = 2
                yield
            else:
                d0 = 0
            for d in range(d0, G):
                for g in range(G):
                    mm(d, g)
                epilogue(d, *banks[d % nb])
                yield

        def ffn_a_g(chg, kbuf_t, r_kbuf_t, scr, r_scr):
            xk_tiles = [rfull[:, TC * g: TC * (g + 1)] for g in range(G)]

            def ep_kact(d, bank, r_bank):
                sl = d % 2
                k.emit("scalar", lambda e, bank=bank, sl=sl, scr=scr: e.activation(
                    scr[:, 512 * sl: 512 * (sl + 1)], bank[:], AF.Relu),
                    reads=[r_bank], writes=[r_scr[sl]])
                k.emit("vector", lambda e, d=d, sl=sl, scr=scr, kbuf_t=kbuf_t: e.tensor_tensor(
                    kbuf_t[:, TC * d: TC * (d + 1)], scr[:, 512 * sl: 512 * (sl + 1)],
                    scr[:, 512 * sl: 512 * (sl + 1)], op=AluOpType.mult),
                    reads=[r_scr[sl]], writes=[r_kbuf_t[d]])

            banks = main_banks if kbuf_t is kact else pre_banks
            yield from proj_g(wt["wk"], r_w["wk"], r_rfull, xk_tiles, banks, ep_kact, gfirst=True)

        def ffn_gates_g(chg, banks, scr, r_scr):
            xr_tiles = [xrb[:, TC * g: TC * (g + 1)] for g in range(G)]
            for d in range(G):
                sl = d % 2
                br, r_br = banks[d % len(banks)]
                for g in range(G):
                    k.emit("tensor", lambda e, d=d, g=g, br=br, rhs=xr_tiles[g]: e.matmul(
                        br[:], wtile(wt["wr"], g, d), rhs,
                        start=(g == 0), stop=(g == G - 1)),
                        reads=[r_xr[g], r_w["wr"]], writes=[r_br])
                k.emit("scalar", lambda e, br=br, sl=sl: e.activation(
                    dscr[:, 512 * sl: 512 * (sl + 1)], br[:], AF.Tanh, scale=0.5),
                    reads=[r_br], writes=[r_dscr[sl]])
                k.emit("vector", lambda e, d=d, sl=sl: e.tensor_scalar(
                    rfull[:, TC * d: TC * (d + 1)], in0=dscr[:, 512 * sl: 512 * (sl + 1)],
                    scalar1=0.25, scalar2=0.25, op0=AluOpType.mult, op1=AluOpType.add),
                    reads=[r_dscr[sl]], writes=[r_rfull[d]])
                yield

        def ffn_vb_g(chg, kbuf_t, r_kbuf_t, banks):
            par = chg % 2
            xres, r_xres = xres2[par], r_xres2[par]
            kact_tiles = [kbuf_t[:, TC * g: TC * (g + 1)] for g in range(G)]
            for d in range(G):
                sl = d % 2
                bv, r_bv = banks[d % len(banks)]
                for g in range(G):
                    k.emit("tensor", lambda e, d=d, g=g, bv=bv, rhs=kact_tiles[g]: e.matmul(
                        bv[:], wtile(wt["wv"], g, d), rhs,
                        start=(g == 0), stop=(g == G - 1)),
                        reads=[r_kbuf_t[g], r_w["wv"]], writes=[r_bv])
                k.emit("vector", lambda e, d=d, bv=bv, sl=sl: e.tensor_tensor(
                    fscr[:, 512 * sl: 512 * (sl + 1)], rfull[:, TC * d: TC * (d + 1)],
                    bv[:], op=AluOpType.mult),
                    reads=[r_rfull[d], r_bv], writes=[r_fscr[sl]])
                k.emit("gpsimd", lambda e, d=d, sl=sl, xres=xres: e.tensor_tensor(
                    xres[:, TC * d: TC * (d + 1)], xres[:, TC * d: TC * (d + 1)],
                    fscr[:, 512 * sl: 512 * (sl + 1)], op=AluOpType.add),
                    reads=[r_fscr[sl]], writes=[r_xres[d]])
                yield

        def drain(gen):
            for _ in gen:
                pass

        def ln_phase(which, chg):
            drain(ln_phase_g(which, chg))

        def proj(*a, **kw):
            drain(proj_g(*a, **kw))

        def qkv_phase(chg, lngen=iter(())):
            def proj_w(*a, **kw):
                for _ in proj_g(*a, **kw):
                    next(lngen, None)
            xh_tiles = [xhat[:, (TC + 1) * g + 1: (TC + 1) * (g + 1)] for g in range(G)]

            def ep_q(d, bank, r_bank):
                k.emit("vector", lambda e, d=d, bank=bank: e.tensor_copy(
                    qbuf[:, TC * d: TC * (d + 1)], bank[:]),
                    reads=[r_bank], writes=[r_q[d]])

            proj_w(wb[0], r_wb[0], r_xhat, xh_tiles, main_banks, ep_q, gfirst=True)
            ld_stream("wva", 0)

            if chg % n_ch != 0:
                k.emit("vector", lambda e: e.tensor_copy(
                    kbuf[:].rearrange("p (g c) -> p g c", c=TC + 128)[:, :, 0:128],
                    kbuf[:].rearrange("p (g c) -> p g c", c=TC + 128)[:, :, TC: TC + 128]),
                    reads=list(r_kbuf), writes=list(r_kbuf))

            def ep_k(d, bank, r_bank):
                k.emit("scalar", lambda e, d=d, bank=bank: e.copy(
                    kbuf[:, (TC + 128) * d + 128: (TC + 128) * (d + 1)], bank[:]),
                    reads=[r_bank], writes=[r_kbuf[d]])

            proj_w(wb[1], r_wb[1], r_xhat, xh_tiles, main_banks, ep_k)
            ld_stream("wo", 1)
            next(lngen, None)

            vtm_banks = [(pj[0], r_pj[0]), (pj[1], r_pj[1]),
                         (psT[0], r_psT[0]), (psT[1], r_psT[1])]
            for tt in range(NCH):
                slot = (NCH * chg + tt) % 5
                for half in range(2):
                    bank, r_bank = vtm_banks[(2 * tt + half) % 4]
                    for g in range(G):
                        k.emit("tensor", lambda e, tt=tt, half=half, g=g, bank=bank: e.matmul(
                            bank[:], xhat[:, (TC + 1) * g + 1 + 128 * tt:
                                          (TC + 1) * g + 1 + 128 * (tt + 1)],
                            wb[0][:, 1024 * g + 512 * half: 1024 * g + 512 * (half + 1)],
                            start=(g == 0), stop=(g == G - 1)),
                            reads=[r_xhat[g], r_wb[0]], writes=[r_bank])
                    k.emit("vector", lambda e, slot=slot, half=half, bank=bank: e.tensor_copy(
                        vring[slot][0:128, 512 * half: 512 * (half + 1)], bank[:]),
                        reads=[r_bank], writes=[r_vring[slot]])
                    next(lngen, None)

        def attn_phase(chg, weave=None):
            par = chg % 2
            xres, r_xres = xres2[par], r_xres2[par]
            first = (chg % n_ch == 0)
            units = ([] if first else [-1]) + [0, 1, 2, 3]
            sbanks = [(psT[0], r_psT[0]), (psT[1], r_psT[1]),
                      (pj[0], r_pj[0]), (pj[1], r_pj[1])]
            uctr = [0]
            pend = [None]
            zeroed = [False]

            def emit_scores(hg, jt):
                i = uctr[0]
                uctr[0] += 1
                slb, esl = i % 4, i % 2
                woff = max(jt, 0) * 128
                w = min(jt + 2, 4) * 128 - woff
                kcol = (TC + 128) * hg + 128 * (jt + 1)
                bnk, r_bnk = sbanks[slb]
                ma = k.emit("tensor", lambda e, bnk=bnk, kcol=kcol, woff=woff,
                            w=w, hg=hg: e.matmul(
                    bnk[:, 0: w],
                    kbuf[0:64, kcol: kcol + 128],
                    qbuf[0:64, TC * hg + woff: TC * hg + woff + w],
                    start=True, stop=True),
                    reads=[r_kbuf[hg], r_q[hg]], writes=[r_bnk])
                k.emit("tensor", lambda e, bnk=bnk, kcol=kcol, woff=woff,
                       w=w, hg=hg: e.matmul(
                    bnk[:, w: 2 * w],
                    kbuf[64:128, kcol: kcol + 128],
                    qbuf[64:128, TC * hg + woff: TC * hg + woff + w],
                    start=True, stop=True),
                    reads=[r_kbuf[hg], r_q[hg]], writes=[r_bnk],
                    extra_waits=[(ma[0], ma[1])])
                k.emit("scalar", lambda e, bnk=bnk, esl=esl, w=w: e.activation(
                    ebuf[:, 512 * esl: 512 * esl + 2 * w], bnk[:, 0: 2 * w],
                    AF.Exp, bias=c_nb4[:], scale=0.125),
                    reads=[r_bnk, r_const], writes=[r_ebuf[esl]])
                mask = mf[:, 0:512] if w == 256 else (mh[:] if jt == -1 else ml[:])
                k.emit("vector", lambda e, esl=esl, w=w, mask=mask: e.tensor_tensor(
                    ebuf[:, 512 * esl: 512 * esl + 2 * w],
                    ebuf[:, 512 * esl: 512 * esl + 2 * w], mask, op=AluOpType.mult),
                    reads=[r_ebuf[esl], r_masks], writes=[r_ebuf[esl]])
                return (hg, jt, esl, woff, w)

            def emit_dpv(item):
                hg, jt, esl, woff, w = item
                if not zeroed[0]:
                    zeroed[0] = True
                    k.emit("tensor", lambda e: e.matmul(
                        pdT[:], zz[:], mf[:], start=True, stop=True),
                        reads=[r_const, r_masks], writes=[r_pdT])
                    k.emit("tensor", lambda e: e.matmul(
                        pyT[:], zz[:], mf[:], start=True, stop=True),
                        reads=[r_const, r_masks], writes=[r_pyT])
                vslot = (NCH * chg + jt) % 5 if jt >= 0 else (NCH * chg - 1) % 5
                for cb in range(w // 128):
                    qb = woff + 128 * cb
                    for hh in range(2):
                        tp = (0, 64 * hh)
                        eview = (512 * esl + w * hh + 128 * cb)
                        k.emit("tensor", lambda e, hh=hh, eview=eview, qb=qb, tp=tp: e.matmul(
                            pdT[64 * hh: 64 * (hh + 1), qb: qb + 128],
                            ones[:, 0:64], ebuf[:, eview: eview + 128],
                            start=False, stop=True, tile_position=tp),
                            reads=[r_ebuf[esl], r_const], writes=[r_pdT])
                        k.emit("tensor", lambda e, hh=hh, eview=eview, qb=qb, tp=tp,
                               vslot=vslot, hg=hg: e.matmul(
                            pyT[64 * hh: 64 * (hh + 1), qb: qb + 128],
                            vring[vslot][:, 64 * (2 * hg + hh): 64 * (2 * hg + hh + 1)],
                            ebuf[:, eview: eview + 128],
                            start=False, stop=True, tile_position=tp),
                            reads=[r_ebuf[esl], r_vring[vslot]], writes=[r_pyT])

            def emit_group_tail(hg):
                asl = hg % 2
                k.emit("scalar", lambda e, asl=asl: e.activation(
                    rscr[:, 512 * asl: 512 * (asl + 1)], pdT[:], AF.Ln),
                    reads=[r_pdT], writes=[r_rscr[asl]])
                k.emit("scalar", lambda e, asl=asl: e.activation(
                    rscr2[:, 512 * asl: 512 * (asl + 1)], rscr[:, 512 * asl: 512 * (asl + 1)],
                    AF.Exp, scale=-1.0),
                    reads=[r_rscr[asl]], writes=[r_rscr2[asl]])
                k.emit("vector", lambda e, hg=hg, asl=asl: e.tensor_tensor(
                    kact[:, TC * hg: TC * (hg + 1)], pyT[:],
                    rscr2[:, 512 * asl: 512 * (asl + 1)], op=AluOpType.mult),
                    reads=[r_pyT, r_rscr2[asl]], writes=[r_kact[hg]])

            uidx = [0]
            for hg in range(G):
                for jt in units:
                    item = emit_scores(hg, jt)
                    if pend[0] is not None:
                        ph = pend[0][0]
                        emit_dpv(pend[0])
                        if ph != hg:
                            emit_group_tail(ph)
                            zeroed[0] = False
                    pend[0] = item
                    if weave is not None:
                        weave(uidx[0])
                    uidx[0] += 1
            emit_dpv(pend[0])
            emit_group_tail(G - 1)
            pend[0] = None

            y_tiles = [kact[:, TC * g: TC * (g + 1)] for g in range(G)]

            def ep_wo(d, bank, r_bank):
                k.emit("vector", lambda e, d=d, bank=bank, xres=xres: e.tensor_tensor(
                    xres[:, TC * d: TC * (d + 1)], xres[:, TC * d: TC * (d + 1)],
                    bank[:], op=AluOpType.add),
                    reads=[r_bank], writes=[r_xres[d]])

            proj(wb[1], r_wb[1], r_kact, y_tiles,
                 [(pj[0], r_pj[0]), (pj[1], r_pj[1]), (psT[0], r_psT[0]), (psT[1], r_psT[1])],
                 ep_wo)

        def emit_dumps():
            if not dumps:
                return
            registry = {
                "xrb": (xrb, r_xr, G * TC), "xkb": (xkb, r_xk, G * TC),
                "kbuf": (kbuf, r_kbuf, G * (TC + 128)),
                "kact": (kact, r_kact, G * TC),
                "xhat": (xhat, r_xhat, G * (TC + 1)),
                "vring0": (vring[0], [r_vring[0]], 1024),
                "ebuf": (ebuf, r_ebuf, 1024),
            }
            dbg_view = out_h[1].rearrange("(g p) t -> p g t", p=128)
            gidx = 0
            for name in dumps:
                buf, rs, ncols = registry[name]
                for p0 in range(0, ncols, 2048):
                    plen = min(2048, ncols - p0)
                    k.emit("gpsimd", lambda e, buf=buf, p0=p0, plen=plen, gidx=gidx: e.dma_start(
                        out=dbg_view[:, gidx: gidx + 1, 0: plen],
                        in_=buf[:, p0: p0 + plen].rearrange("p (o t) -> p o t", o=1)),
                        reads=list(rs), dma="dst")
                    gidx += 1

        pre_banks = [(pst[0], r_pst[0]), (pst[1], r_pst[1])]
        main_banks = [(pj[0], r_pj[0]), (pj[1], r_pj[1]),
                      (psT[0], r_psT[0]), (psT[1], r_psT[1])]

        def prelude1_g(chg):
            if "f1" in phases:
                yield from ffn_a_g(chg, xhat, r_xhat, fscr, r_fscr)
                yield from ffn_gates_g(chg, pre_banks, fscr, r_fscr)
                yield from ffn_vb_g(chg, xhat, r_xhat, pre_banks)

        ld_x(0)
        ld_stream("wq", 0)
        ld_stream("wka", 1)
        if "ln1" in phases:
            drain(ln_phase_g(1, 0))
        drain(prelude1_g(0))
        for chg in range(nchunks):
            if chg + 1 < nchunks:
                ld_x(chg + 1)
            if "ln2" in phases:
                ln_phase(2, chg)
            lngen = (ln_phase_g(1, chg + 1)
                     if (chg + 1 < nchunks and "ln1" in phases) else iter(()))
            if "qkv" in phases:
                qkv_phase(chg, lngen)
            drain(lngen)
            pgen = prelude1_g(chg + 1) if chg + 1 < nchunks else iter(())
            wq_done = [False]

            def weave(uidx, pgen=pgen, wq_done=wq_done):
                if uidx == 2 and not wq_done[0]:
                    wq_done[0] = True
                    ld_stream("wq", 0)
                if uidx >= 2:
                    next(pgen, None)

            if "attn" in phases:
                attn_phase(chg, weave=weave)
                drain(pgen)
                if chg + 1 < nchunks:
                    ld_stream("wka", 1)
            else:
                if chg + 1 < nchunks:
                    ld_stream("wq", 0)
                    ld_stream("wka", 1)
                drain(pgen)
            if "ln3" in phases:
                ln_phase(3, chg)
            if "f2" in phases:
                drain(ffn_a_g(chg, kact, r_kact, rscr, r_rscr))
                drain(ffn_gates_g(chg, main_banks, rscr, r_rscr))
                drain(ffn_vb_g(chg, kact, r_kact, main_banks))
            st_x(chg)

        emit_dumps()

        total = k.counts["dst"]
        k.ops["sync"].append((None, [("dst", total)], None, 0))

        block = ctx.enter_context(nc.Block())
        k.realize(block, sems)

    return nc


_NC_CACHE = None


def _get_program():
    global _NC_CACHE
    if _NC_CACHE is None:
        _NC_CACHE = build_program()
    return _NC_CACHE


def kernel(L0_attn_x, L0_attn_Wq, L0_attn_Wk, L0_attn_Wv, L0_attn_Wo,
           L0_ffn_W_r, L0_ffn_W_k, L0_ffn_W_v, L0_ffn_mu_r, L0_ffn_mu_k):
    x = np.asarray(L0_attn_x, dtype=np.float32)

    def wt_(w):
        return np.ascontiguousarray(np.asarray(w, dtype=np.float32).T)

    shared = {
        "wr": wt_(L0_ffn_W_r), "wk": wt_(L0_ffn_W_k), "wv": wt_(L0_ffn_W_v),
        "wq": wt_(L0_attn_Wq), "wka": wt_(L0_attn_Wk), "wva": wt_(L0_attn_Wv),
        "wo": wt_(L0_attn_Wo),
        "mur": np.ascontiguousarray(np.asarray(L0_ffn_mu_r, np.float32).reshape(G, 128).T),
        "muk": np.ascontiguousarray(np.asarray(L0_ffn_mu_k, np.float32).reshape(G, 128).T),
    }
    r = np.arange(128)[:, None]
    c = np.arange(128)[None, :]
    triu = (c >= r).astype(np.float32)
    stril = (c < r).astype(np.float32)
    shared["mf"] = np.ascontiguousarray(np.concatenate([triu, stril, triu, stril], axis=1))
    shared["mh"] = np.ascontiguousarray(np.concatenate([stril, stril], axis=1))
    shared["ml"] = np.ascontiguousarray(np.concatenate([triu, triu], axis=1))
    shared["onr"] = np.ones((128, 128), np.float32)

    in_maps = []
    for core in range(NCORES):
        xc = np.ascontiguousarray(x[SEQ * core: SEQ * (core + 1)].transpose(0, 2, 1))
        in_maps.append({"x": xc, **shared})

    nc = _get_program()
    res = run_bass_kernel_spmd(nc, in_maps, core_ids=list(range(NCORES)))
    outs = [np.asarray(res.results[i]["out"]).transpose(0, 2, 1) for i in range(NCORES)]
    return np.ascontiguousarray(np.concatenate(outs, axis=0)).astype(np.float32)
